# version 14
# baseline (speedup 1.0000x reference)
import numpy as np
import concourse.bass as bass
import concourse.mybir as mybir
from concourse.bass_utils import run_bass_kernel_spmd

F32 = mybir.dt.float32
BF16 = mybir.dt.bfloat16
AF = mybir.ActivationFunctionType
ALU = mybir.AluOpType
AX = mybir.AxisListType

D = 1024
KC = D // 128
NE = 12
DA = 64
PD = 256
TPC = 8
TW = TPC * 128
BIG = 1.0e30
INV_TAU = 1.0 / (1.0 + 1e-8)
LN_EPS = 1e-5
Z_EPS = 1e-8


class Sem:
    def __init__(self, nc, name):
        self.h = nc.semaphore(name).__enter__()
        self.cnt = 0

    def inc(self, instr, n=1):
        instr.then_inc(self.h, n)
        self.cnt += n
        return instr


class Builder:

    def __init__(self, nc, n_tok):
        self.nc = nc
        self.n_tok = n_tok
        self.n_tiles = n_tok // 128
        self.n_chunks = self.n_tiles // TPC
        t = {}
        self.t = t
        t["x"] = nc.dram_tensor("x", [n_tok, D], F32, kind="ExternalInput")
        t["o"] = nc.dram_tensor("o", [n_tok, D], F32, kind="ExternalOutput")
        t["xbuf"] = nc.dram_tensor("xbuf", [n_tok, D], F32, kind="Internal")
        t["w1t"] = nc.dram_tensor("w1t", [NE, D, D], F32, kind="ExternalInput")
        t["w2t"] = nc.dram_tensor("w2t", [NE, D, D], F32, kind="ExternalInput")
        t["pwt"] = nc.dram_tensor("pwt", [D, DA], F32, kind="ExternalInput")
        t["ut"] = nc.dram_tensor("ut", [DA, 3 * NE], F32, kind="ExternalInput")
        t["b2"] = nc.dram_tensor("b2", [NE, D], F32, kind="ExternalInput")
        t["qp"] = nc.dram_tensor("qp", [D, PD], F32, kind="ExternalInput")
        t["kp"] = nc.dram_tensor("kp", [D, PD], F32, kind="ExternalInput")
        t["ident"] = nc.dram_tensor("ident", [128, 128], F32, kind="ExternalInput")
        sb = nc.alloc_sbuf_tensor
        self.xc = [sb("xc0", [128, TPC * D], F32), sb("xc1", [128, TPC * D], F32)]
        self.A = sb("A", [128, TPC * D], F32)
        self.Bhi = sb("Bhi", [128, KC * TW], BF16)
        self.Blo = sb("Blo", [128, KC * TW], BF16)
        self.wb = [sb("wb0", [128, 2048], F32), sb("wb1", [128, 2048], F32)]
        self.att = sb("att", [128, 2 * TW], F32)
        self.kvs = sb("kvs", [128, 2 * D], F32)
        self.ident_s = sb("ident_s", [128, 128], F32)
        self.pwhi = sb("pwhi", [128, KC * DA], BF16)
        self.pwlo = sb("pwlo", [128, KC * DA], BF16)
        self.ut_s = sb("ut_s", [DA, 3 * NE], F32)
        self.b2_s = sb("b2_s", [NE, D], F32)
        self.qphi = sb("qphi", [128, KC * PD], BF16)
        self.qplo = sb("qplo", [128, KC * PD], BF16)
        self.kphi = sb("kphi", [128, KC * PD], BF16)
        self.kplo = sb("kplo", [128, KC * PD], BF16)
        self.C1 = sb("C1", [128, TPC * NE], F32)
        self.C2 = sb("C2", [128, TPC * NE], F32)
        self.C3 = sb("C3", [128, TPC * NE], F32)
        self.C3T = sb("C3T", [NE, TPC * 128], F32)
        self.xa_s = sb("xa_s", [DA, TW], F32)
        self.z_s = sb("z_s", [128, TPC * 3 * NE], F32)
        self.mcol = sb("mcol", [128, self.n_tiles], F32)
        self.rcol = sb("rcol", [128, self.n_tiles], F32)
        self.zcol = sb("zcol", [128, TPC], F32)
        self.scr = sb("scr", [128, 16], F32)
        self.g12 = sb("g12", [128, 4 * TPC * 3], F32)
        self.e144 = sb("e144", [128, TPC * 3 * NE], F32)
        self.zt144 = sb("zt144", [128, TPC * 3 * NE], F32)
        self.ones = sb("ones", [128, 1], F32)
        self.gtmp = sb("gtmp", [128, 4 * 128], F32)
        self.pA = nc.alloc_psum_tensor("pA", [128, 2048], F32)
        self.pB = nc.alloc_psum_tensor("pB", [128, 2048], F32)
        self.sv = Sem(nc, "sv")
        self.ss = Sem(nc, "ss")
        self.st = Sem(nc, "st_t")
        self.sw = [Sem(nc, "sw0"), Sem(nc, "sw1")]
        self.sin = Sem(nc, "sin")
        self.sxc = [Sem(nc, "sxc0"), Sem(nc, "sxc1")]
        self.sst = [Sem(nc, "sst0"), Sem(nc, "sst1")]
        self.wb_free_at = [0, 0]
        self.xc_free = [[], []]
        self.xc_ready = [0, 0]

    def wv(self, eng, val=None):
        eng.wait_ge(self.sv.h, self.sv.cnt if val is None else val)

    def ws(self, eng, val=None):
        eng.wait_ge(self.ss.h, self.ss.cnt if val is None else val)

    def wt(self, eng, val=None):
        eng.wait_ge(self.st.h, self.st.cnt if val is None else val)

    def load_xc(self, src_dram, chunk):
        nc = self.nc
        p = chunk % 2
        for sem, val in self.xc_free[p]:
            nc.sync.wait_ge(sem, val)
        self.xc_free[p] = []
        rows = src_dram[chunk * TW:(chunk + 1) * TW, :]
        src = rows.rearrange("(t p) f -> p t f", p=128)
        dst = self.xc[p][:].rearrange("p (t f) -> p t f", f=D)
        self.sxc[p].inc(nc.sync.dma_start(out=dst, in_=src), 16)
        self.xc_ready[p] = self.sxc[p].cnt

    def store_xc(self, dst_dram, chunk):
        nc = self.nc
        p = chunk % 2
        nc.sync.wait_ge(self.sv.h, self.sv.cnt)
        rows = dst_dram[chunk * TW:(chunk + 1) * TW, :]
        dst = rows.rearrange("(t p) f -> p t f", p=128)
        src = self.xc[p][:].rearrange("p (t f) -> p t f", f=D)
        self.sst[p].inc(nc.sync.dma_start(out=dst, in_=src), 16)
        self.xc_free[p] = [(self.sst[p].h, self.sst[p].cnt)]

    def emit_loads(self):
        nc, t = self.nc, self.t
        self.sin.inc(nc.sync.dma_start(out=self.ident_s[:], in_=t["ident"][:]), 16)
        self.sin.inc(nc.sync.dma_start(out=self.ut_s[:], in_=t["ut"][:]), 16)
        self.sin.inc(nc.sync.dma_start(out=self.b2_s[:], in_=t["b2"][:]), 16)
        for eng in (nc.vector, nc.scalar, nc.tensor):
            eng.wait_ge(self.sin.h, self.sin.cnt)
        self.sv.inc(nc.vector.memset(self.ones[:], 1.0))

    def emit_prologue(self):
        nc, t = self.nc, self.t
        V, S = nc.vector, nc.scalar
        for name, hi, lo, cols, j in (("qp", self.qphi, self.qplo, KC * PD, PD),
                                      ("kp", self.kphi, self.kplo, KC * PD, PD),
                                      ("pwt", self.pwhi, self.pwlo, KC * DA, DA)):
            stage = self.A[:, 0:cols]
            src = t[name][:].rearrange("(kc p) j -> p kc j", p=128)
            nc.sync.wait_ge(self.sv.h, self.sv.cnt)
            nc.sync.wait_ge(self.ss.h, self.ss.cnt)
            self.sin.inc(nc.sync.dma_start(
                out=stage.rearrange("p (kc j) -> p kc j", j=j), in_=src), 16)
            S.wait_ge(self.sin.h, self.sin.cnt)
            self.ss.inc(S.copy(hi[:], stage))
            V.wait_ge(self.sin.h, self.sin.cnt)
            self.ws(V)
            self.sv.inc(V.tensor_sub(lo[:], stage, hi[:]))

    def emit_ln(self, chunk, save_stats=False, use_saved=False, a_free_t=None):
        nc = self.nc
        V, S = nc.vector, nc.scalar
        p = chunk % 2
        xv = self.xc[p]
        for tt in range(TPC):
            g = chunk * TPC + tt
            xt = xv[:, tt * D:(tt + 1) * D]
            At = self.A[:, tt * D:(tt + 1) * D]
            if use_saved:
                m = self.mcol[:, g:g + 1]
                r = self.rcol[:, g:g + 1]
                if tt == 0:
                    V.wait_ge(self.sxc[p].h, self.xc_ready[p])
                    self.wt(V, a_free_t)
                    self.ws(V)
                    self.wv(V)
                self.sv.inc(V.tensor_scalar(At, xt, m, r, ALU.subtract, ALU.mult))
                continue
            m = self.mcol[:, g:g + 1] if save_stats else self.scr[:, 0:1]
            r = self.rcol[:, g:g + 1] if save_stats else self.scr[:, 1:2]
            sumc = self.scr[:, 2:3]
            sqc = self.scr[:, 3:4]
            varc = self.scr[:, 4:5]
            msq = self.scr[:, 5:6]
            if tt == 0:
                V.wait_ge(self.sxc[p].h, self.xc_ready[p])
                self.wv(V)
            self.sv.inc(V.reduce_sum(sumc, xt, axis=AX.X))
            if tt == 0:
                S.wait_ge(self.sxc[p].h, self.xc_ready[p])
                self.wt(S, a_free_t)
            self.wv(S)
            self.ss.inc(S.activation(At, xt, AF.Square, accum_out=sqc))
            self.ws(V)
            self.wv(V)
            self.sv.inc(V.tensor_scalar_mul(m, sumc, 1.0 / D))
            self.wv(V)
            self.sv.inc(V.tensor_mul(msq, m, m))
            self.sv.inc(V.tensor_scalar(varc, sqc, 1.0 / D, LN_EPS, ALU.mult, ALU.add))
            self.wv(V)
            self.sv.inc(V.tensor_sub(varc, varc, msq))
            self.wv(S)
            self.ss.inc(S.activation(varc, varc, AF.Ln))
            self.ws(S)
            self.ss.inc(S.activation(r, varc, AF.Exp, scale=-0.5))
            self.ws(V)
            self.sv.inc(V.tensor_scalar(At, xt, m, r, ALU.subtract, ALU.mult))

    def emit_transpose_A_to_B(self, arena, slots):
        nc = self.nc
        T, S, V = nc.tensor, nc.scalar, nc.vector
        slot_free = [None] * len(slots)
        first = True
        for tt in range(TPC):
            for kc in range(KC):
                i = tt * KC + kc
                sl = i % len(slots)
                base = slots[sl]
                pview = arena[:, base: base + 128]
                src = self.A[:, tt * D + kc * 128: tt * D + (kc + 1) * 128]
                if first:
                    self.wv(T)
                    self.ws(T)
                    first = False
                elif slot_free[sl] is not None:
                    self.ws(T, slot_free[sl][0])
                    self.wv(T, slot_free[sl][1])
                self.st.inc(T.transpose(pview, src, self.ident_s[:]))
                self.wt(S)
                hi = self.Bhi[:, kc * TW + tt * 128: kc * TW + (tt + 1) * 128]
                lo = self.Blo[:, kc * TW + tt * 128: kc * TW + (tt + 1) * 128]
                self.ss.inc(S.copy(hi, pview))
                self.wt(V)
                self.ws(V)
                self.sv.inc(V.tensor_sub(lo, pview, hi))
                slot_free[sl] = (self.ss.cnt, self.sv.cnt)

    def emit_transpose_gelu_A_to_B(self, arena, slots):
        nc = self.nc
        T, S, V = nc.tensor, nc.scalar, nc.vector
        slot_free = [None] * len(slots)
        first = True
        for tt in range(TPC):
            for kc in range(KC):
                i = tt * KC + kc
                sl = i % len(slots)
                base = slots[sl]
                pview = arena[:, base: base + 128]
                gt = self.gtmp[:, sl * 128:(sl + 1) * 128]
                src = self.A[:, tt * D + kc * 128: tt * D + (kc + 1) * 128]
                if first:
                    self.wv(T)
                    self.ws(T)
                    first = False
                elif slot_free[sl] is not None:
                    self.ws(T, slot_free[sl][0])
                    self.wv(T, slot_free[sl][1])
                self.st.inc(T.transpose(pview, src, self.ident_s[:]))
                self.wt(S)
                hi = self.Bhi[:, kc * TW + tt * 128: kc * TW + (tt + 1) * 128]
                lo = self.Blo[:, kc * TW + tt * 128: kc * TW + (tt + 1) * 128]
                self.ss.inc(S.activation(gt, pview, AF.Gelu))
                self.ws(S)
                self.ss.inc(S.copy(hi, gt))
                self.ws(V)
                self.sv.inc(V.tensor_sub(lo, gt, hi))
                slot_free[sl] = (self.ss.cnt, self.sv.cnt)

    def _mm3(self, T, out, lhi, llo, rhi, rlo, start, stop):
        T.matmul(out, lhi, rhi, start=start, stop=False)
        T.matmul(out, lhi, rlo, start=False, stop=False)
        return T.matmul(out, llo, rhi, start=False, stop=stop)

    def emit_routing(self, chunk):
        nc = self.nc
        T, V, S = nc.tensor, nc.vector, nc.scalar
        self.ws(T)
        self.wv(T)
        ins = None
        for kc in range(KC):
            for th in range(2):
                xap = self.pB[0:DA, th * 512:(th + 1) * 512]
                ins = self._mm3(T, xap,
                                self.pwhi[:, kc * DA:(kc + 1) * DA],
                                self.pwlo[:, kc * DA:(kc + 1) * DA],
                                self.Bhi[:, kc * TW + th * 512: kc * TW + (th + 1) * 512],
                                self.Blo[:, kc * TW + th * 512: kc * TW + (th + 1) * 512],
                                start=(kc == 0), stop=(kc == KC - 1))
        self.st.inc(ins)
        self.wt(V)
        self.sv.inc(V.tensor_copy(self.xa_s[:], self.pB[0:DA, 0:TW]))
        zp = self.pB[:, 1024:1024 + TPC * 3 * NE]
        self.wv(T)
        for tt in range(TPC):
            for r in range(3):
                self.st.inc(T.matmul(zp[:, tt * 36 + r * NE: tt * 36 + (r + 1) * NE],
                                     self.xa_s[:, tt * 128:(tt + 1) * 128],
                                     self.ut_s[:, r * NE:(r + 1) * NE],
                                     start=True, stop=True))
        self.wt(V)
        self.sv.inc(V.tensor_copy(self.z_s[:], zp))
        NG = TPC * 3
        z3 = self.z_s[:].rearrange("p (g w) -> p g w", w=NE)
        e3 = self.e144[:].rearrange("p (g w) -> p g w", w=NE)
        zt3 = self.zt144[:].rearrange("p (g w) -> p g w", w=NE)
        M1 = self.g12[:, 0:NG]
        M3 = self.g12[:, NG:2 * NG]
        DEN = self.g12[:, 2 * NG:3 * NG]
        RD = self.g12[:, 3 * NG:4 * NG]
        M1b = M1.unsqueeze(2).broadcast_to([128, NG, NE])
        M3b = M3.unsqueeze(2).broadcast_to([128, NG, NE])
        self.wv(V)
        self.sv.inc(V.reduce_max(M1, z3, axis=AX.X))
        self.wv(V)
        self.sv.inc(V.tensor_tensor(e3, z3, M1b, ALU.is_ge))
        self.wv(V)
        self.sv.inc(V.scalar_tensor_tensor(zt3, e3, -BIG, z3, ALU.mult, ALU.add))
        self.wv(V)
        self.sv.inc(V.reduce_max(M3, zt3, axis=AX.X))
        self.wv(V)
        self.sv.inc(V.tensor_tensor(e3, zt3, M3b, ALU.is_ge))
        self.wv(V)
        self.sv.inc(V.scalar_tensor_tensor(zt3, e3, -BIG, zt3, ALU.mult, ALU.add))
        self.wv(V)
        self.sv.inc(V.reduce_max(M3, zt3, axis=AX.X))
        self.wv(V)
        self.sv.inc(V.tensor_tensor(zt3, z3, M1b, ALU.subtract))
        self.sv.inc(V.tensor_tensor(e3, z3, M3b, ALU.is_ge))
        self.wv(S)
        self.ss.inc(S.activation(self.zt144[:], self.zt144[:], AF.Exp, scale=INV_TAU))
        self.ws(V)
        self.sv.inc(V.tensor_mul(self.zt144[:], self.zt144[:], self.e144[:]))
        self.wv(V)
        self.sv.inc(V.reduce_sum(DEN, zt3, axis=AX.X))
        self.wv(V)
        self.sv.inc(V.reciprocal(RD, DEN))
        self.wv(V)
        self.sv.inc(V.tensor_scalar_mul(DEN, RD, 0.5))
        self.wv(V)
        zt4 = self.zt144[:].rearrange("p (t r w) -> p t r w", r=3, w=NE)
        for r in range(3):
            C = (self.C1, self.C2, self.C3)[r]
            sc = RD if r == 0 else DEN
            sc_r = sc.rearrange("p (t r) -> p t r", r=3)[:, :, r]
            sc_rb = sc_r.unsqueeze(2).broadcast_to([128, TPC, NE])
            self.sv.inc(V.tensor_tensor(
                C[:].rearrange("p (t w) -> p t w", w=NE), zt4[:, :, r, :], sc_rb,
                ALU.mult))

    def load_quarter(self, wdram, pair, q):
        nc = self.nc
        S, V = nc.scalar, nc.vector
        e, oh = pair // 2, pair % 2
        kb, half = q // 2, q % 2
        gq = pair * 4 + q
        slot = gq % 2
        r0 = kb * 512 + half * 256
        src = wdram[e, r0:r0 + 256, oh * 512:(oh + 1) * 512]
        src = src.rearrange("(k p) o -> p k o", p=128)
        f32 = self.wb[slot][:, 0:1024]
        wbB = self.wb[slot][:].bitcast(BF16)
        hi = wbB[:, 2048:3072]
        lo = wbB[:, 3072:4096]
        nc.sync.wait_ge(self.st.h, self.wb_free_at[slot])
        self.sw[slot].inc(nc.sync.dma_start(
            out=f32.rearrange("p (k o) -> p k o", o=512), in_=src), 16)
        S.wait_ge(self.sw[slot].h, self.sw[slot].cnt)
        self.wt(S, self.wb_free_at[slot])
        self.ss.inc(S.copy(hi, f32))
        V.wait_ge(self.sw[slot].h, self.sw[slot].cnt)
        self.ws(V)
        self.wt(V, self.wb_free_at[slot])
        self.sv.inc(V.tensor_sub(lo, f32, hi))
        self.split_done[slot] = (self.ss.cnt, self.sv.cnt)

    def preload_mixture(self, wdram):
        self.split_done = [None, None]
        self.load_quarter(wdram, 0, 0)
        self.load_quarter(wdram, 0, 1)

    def emit_mixture(self, chunk, wdram, C, dest_is_xc, first_clears,
                     preloaded=False, hooks=None):
        nc = self.nc
        T, V = nc.tensor, nc.vector
        xv = self.xc[chunk % 2]

        self.wv(T)
        self.ws(T)
        if not preloaded:
            self.preload_mixture(wdram)
        evac_after = {}
        for pair in range(2 * NE):
            e, oh = pair // 2, pair % 2
            for q in range(4):
                gq = pair * 4 + q
                slot = gq % 2
                kb, half = q // 2, q % 2
                wbB = self.wb[slot][:].bitcast(BF16)
                sd = self.split_done[slot]
                self.ws(T, sd[0])
                self.wv(T, sd[1])
                for th in range(2):
                    grp = self.pA if th == 0 else self.pB
                    if q == 0 and pair >= 1 and (pair - 1, th) in evac_after:
                        T.wait_ge(self.sv.h, evac_after[(pair - 1, th)])
                    last = None
                    for k4h in range(2):
                        kc = kb * 4 + half * 2 + k4h
                        whi = wbB[:, 2048 + k4h * 512: 2048 + (k4h + 1) * 512]
                        wlo = wbB[:, 3072 + k4h * 512: 3072 + (k4h + 1) * 512]
                        for t4 in range(4):
                            tt = th * 4 + t4
                            bhi = self.Bhi[:, kc * TW + tt * 128: kc * TW + (tt + 1) * 128]
                            blo = self.Blo[:, kc * TW + tt * 128: kc * TW + (tt + 1) * 128]
                            last = self._mm3(T, grp[:, t4 * 512:(t4 + 1) * 512],
                                             bhi, blo, whi, wlo,
                                             start=(kc == 0), stop=(kc == KC - 1))
                    self.st.inc(last)
                    if q == 3:
                        self.wt(V)
                        lastv = None
                        for t4 in range(4):
                            tt = th * 4 + t4
                            ps = grp[:, t4 * 512:(t4 + 1) * 512]
                            if dest_is_xc:
                                dst = xv[:, tt * D + oh * 512: tt * D + oh * 512 + 512]
                            else:
                                dst = self.A[:, tt * D + oh * 512: tt * D + oh * 512 + 512]
                            col = C[:, tt * NE + e: tt * NE + e + 1]
                            if first_clears and e == 0:
                                lastv = V.tensor_scalar(dst, ps, col, None, ALU.mult)
                            else:
                                lastv = V.scalar_tensor_tensor(dst, ps, col, dst,
                                                               ALU.mult, ALU.add)
                        self.sv.inc(lastv)
                        evac_after[(pair, th)] = self.sv.cnt
                self.wb_free_at[slot] = self.st.cnt
                if gq + 2 < 2 * NE * 4:
                    npair, nq = (gq + 2) // 4, (gq + 2) % 4
                    self.load_quarter(wdram, npair, nq)
            if hooks and pair in hooks:
                hooks[pair]()

    def emit_b2_term(self, chunk):
        nc = self.nc
        T, V, S = nc.tensor, nc.vector, nc.scalar
        xv = self.xc[chunk % 2]
        self.wv(T)
        self.ws(T)
        for tt in range(TPC):
            b = 1024 if tt % 2 == 0 else 1536
            pv = self.pB[0:NE, b: b + 128]
            if tt >= 2:
                self.ws(T)
            self.st.inc(T.transpose(pv, self.C3[:, tt * NE:(tt + 1) * NE], self.ident_s[:]))
            self.wt(S)
            self.ss.inc(S.copy(self.C3T[:, tt * 128:(tt + 1) * 128], pv))
        self.ws(T)
        add_done = {}
        for tt in range(TPC):
            for dh in range(2):
                i = tt * 2 + dh
                pv = self.pA[:, (i % 4) * 512: (i % 4) * 512 + 512]
                if i >= 4:
                    self.wv(T, add_done[i - 4])
                self.st.inc(T.matmul(pv, self.C3T[:, tt * 128:(tt + 1) * 128],
                                     self.b2_s[:, dh * 512:(dh + 1) * 512],
                                     start=True, stop=True))
                self.wt(V)
                dst = xv[:, tt * D + dh * 512: tt * D + dh * 512 + 512]
                self.sv.inc(V.tensor_add(dst, dst, pv))
                add_done[i] = self.sv.cnt

    def emit_ffn_chunk(self, chunk, src_dram, dst_dram, ln_done=False):
        if chunk + 1 < self.n_chunks:
            self.load_xc(src_dram, chunk + 1)
        if not ln_done:
            self.emit_ln(chunk)
        self.preload_mixture(self.t["w1t"][:])
        self.emit_transpose_A_to_B(self.pA, [0, 512, 1024, 1536])
        self.emit_routing(chunk)
        self.emit_mixture(chunk, self.t["w1t"][:], self.C1,
                          dest_is_xc=False, first_clears=True, preloaded=True)
        self.preload_mixture(self.t["w2t"][:])
        self.emit_transpose_gelu_A_to_B(self.pA, [0, 512, 1024, 1536])
        self.emit_b2_term(chunk)
        hooks = None
        if chunk + 1 < self.n_chunks:
            hooks = {2: lambda: self.emit_ln(chunk + 1)}
        self.emit_mixture(chunk, self.t["w2t"][:], self.C2,
                          dest_is_xc=True, first_clears=False, preloaded=True,
                          hooks=hooks)
        self.store_xc(dst_dram, chunk)

    def emit_ffn(self, src_dram, dst_dram):
        self.load_xc(src_dram, 0)
        for c in range(self.n_chunks):
            self.emit_ffn_chunk(c, src_dram, dst_dram, ln_done=(c > 0))

    def emit_attn(self):
        nc = self.nc
        T, V, S = nc.tensor, nc.vector, nc.scalar
        t = self.t
        kprime = self.att[:, 0:TPC * PD]
        kv_s0 = self.kvs[:, 0:D]
        kv_s1 = self.kvs[:, D:2 * D]
        ccol = self.scr[:, 10:12]
        self.wv(nc.vector)
        self.sv.inc(V.memset(ccol, 0.0))
        self.load_xc(t["xbuf"][:], 0)
        for c in range(self.n_chunks):
            p = c % 2
            xv = self.xc[p]
            if c + 1 < self.n_chunks:
                self.load_xc(t["xbuf"][:], c + 1)
            if c == 0:
                self.emit_ln(c, save_stats=True)
            self.emit_transpose_A_to_B(self.pB, [1024, 1536])
            a_free = self.st.cnt
            self.ws(T)
            self.wv(T)
            kp_done = {}
            for th in range(2):
                last = None
                for t4 in range(4):
                    tt = th * 4 + t4
                    pv = self.pB[:, t4 * PD:(t4 + 1) * PD]
                    if th == 1:
                        self.ws(T, kp_done[0])
                    for kc in range(KC):
                        last = self._mm3(
                            T, pv,
                            self.Bhi[:, kc * TW + tt * 128: kc * TW + (tt + 1) * 128],
                            self.Blo[:, kc * TW + tt * 128: kc * TW + (tt + 1) * 128],
                            self.kphi[:, kc * PD:(kc + 1) * PD],
                            self.kplo[:, kc * PD:(kc + 1) * PD],
                            start=(kc == 0), stop=(kc == KC - 1))
                    self.st.inc(last)
                self.wt(S)
                for t4 in range(4):
                    tt = th * 4 + t4
                    self.ss.inc(S.activation(kprime[:, tt * PD:(tt + 1) * PD],
                                             self.pB[:, t4 * PD:(t4 + 1) * PD], AF.Exp))
                kp_done[th] = self.ss.cnt
            self.ws(V)
            lastv = None
            for tt in range(TPC):
                g = c * TPC + tt
                lastv = V.tensor_scalar(kprime[:, tt * PD:(tt + 1) * PD],
                                        kprime[:, tt * PD:(tt + 1) * PD],
                                        self.rcol[:, g:g + 1], None, ALU.mult)
            self.sv.inc(lastv)
            self.ws(T)
            self.wv(T)
            last = None
            for ec in range(2):
                for dh in range(2):
                    pv = self.pA[:, (ec * 2 + dh) * 512:(ec * 2 + dh + 1) * 512]
                    for tt in range(TPC):
                        last = T.matmul(pv,
                                        kprime[:, tt * PD + ec * 128: tt * PD + ec * 128 + 128],
                                        xv[:, tt * D + dh * 512: tt * D + dh * 512 + 512],
                                        start=(c == 0 and tt == 0),
                                        stop=(c == self.n_chunks - 1 and tt == TPC - 1))
            self.st.inc(last)
            last = None
            for ec in range(2):
                cps = self.pB[:, ec * 2: ec * 2 + 1]
                for tt in range(TPC):
                    g = c * TPC + tt
                    last = T.matmul(cps,
                                    kprime[:, tt * PD + ec * 128: tt * PD + ec * 128 + 128],
                                    self.mcol[:, g:g + 1],
                                    start=(tt == 0), stop=(tt == TPC - 1))
            self.st.inc(last)
            kv_read_done = self.st.cnt
            if c + 1 < self.n_chunks:
                self.emit_ln(c + 1, save_stats=True, a_free_t=a_free)
            self.wt(V)
            self.sv.inc(V.tensor_add(ccol[:, 0:1], ccol[:, 0:1], self.pB[:, 0:1]))
            self.sv.inc(V.tensor_add(ccol[:, 1:2], ccol[:, 1:2], self.pB[:, 2:3]))
            self.wv(V)
            self.xc_free[p] = [(self.st.h, kv_read_done)]
        self.wt(V)
        self.sv.inc(V.tensor_copy(kv_s0, self.pA[:, 0:1024]))
        self.sv.inc(V.tensor_copy(kv_s1, self.pA[:, 1024:2048]))
        self.wv(V)
        self.sv.inc(V.tensor_scalar(kv_s0, kv_s0, ccol[:, 0:1], None, ALU.subtract))
        self.sv.inc(V.tensor_scalar(kv_s1, kv_s1, ccol[:, 1:2], None, ALU.subtract))
        self.load_xc(t["xbuf"][:], 0)
        for c in range(self.n_chunks):
            p = c % 2
            xv = self.xc[p]
            if c + 1 < self.n_chunks:
                self.load_xc(t["xbuf"][:], c + 1)
            if c == 0:
                self.emit_ln(c, use_saved=True)
            self.emit_transpose_A_to_B(self.pA, [0, 512, 1024, 1536])
            self.ws(T)
            self.wv(T)
            q_done = {}
            for ec in range(2):
                for th in range(2):
                    i = ec * 2 + th
                    pv = self.pB[:, (i % 2) * 512:(i % 2) * 512 + 512]
                    if i >= 2:
                        self.ws(T, q_done[i - 2])
                    last = None
                    for kc in range(KC):
                        last = self._mm3(
                            T, pv,
                            self.qphi[:, kc * PD + ec * 128: kc * PD + ec * 128 + 128],
                            self.qplo[:, kc * PD + ec * 128: kc * PD + ec * 128 + 128],
                            self.Bhi[:, kc * TW + th * 512: kc * TW + (th + 1) * 512],
                            self.Blo[:, kc * TW + th * 512: kc * TW + (th + 1) * 512],
                            start=(kc == 0), stop=(kc == KC - 1))
                    self.st.inc(last)
                    self.wt(S)
                    self.ss.inc(S.activation(
                        self.att[:, ec * TW + th * 512: ec * TW + (th + 1) * 512],
                        pv, AF.Exp))
                    q_done[i] = self.ss.cnt
            self.ws(T)
            last = None
            for tt in range(TPC):
                pv = self.pB[:, 1024 + tt * 8: 1024 + tt * 8 + 1]
                for ec in range(2):
                    last = T.matmul(pv,
                                    self.att[:, ec * TW + tt * 128: ec * TW + (tt + 1) * 128],
                                    self.ones[:], start=(ec == 0), stop=(ec == 1))
            self.st.inc(last)
            self.wt(V)
            for tt in range(TPC):
                zc = self.zcol[:, tt:tt + 1]
                self.sv.inc(V.tensor_scalar_add(
                    zc, self.pB[:, 1024 + tt * 8: 1024 + tt * 8 + 1], Z_EPS))
            self.wv(V)
            for tt in range(TPC):
                zc = self.zcol[:, tt:tt + 1]
                self.sv.inc(V.reciprocal(zc, zc))
            if c + 1 < self.n_chunks:
                self.emit_ln(c + 1, use_saved=True)
            num_done = {}
            for tt in range(TPC):
                for dh in range(2):
                    i = tt * 2 + dh
                    base = 1536 if i % 2 == 0 else 1024
                    pv = self.pB[:, base: base + 512]
                    if i < 2:
                        self.wv(T)
                    else:
                        self.wv(T, num_done[i - 2])
                    for ec in range(2):
                        last = T.matmul(pv,
                                        self.att[:, ec * TW + tt * 128: ec * TW + (tt + 1) * 128],
                                        (kv_s0 if ec == 0 else kv_s1)[:, dh * 512:(dh + 1) * 512],
                                        start=(ec == 0), stop=(ec == 1))
                    self.st.inc(last)
                    self.wt(V)
                    dst = xv[:, tt * D + dh * 512: tt * D + dh * 512 + 512]
                    self.sv.inc(V.scalar_tensor_tensor(dst, pv, self.zcol[:, tt:tt + 1],
                                                       dst, ALU.mult, ALU.add))
                    num_done[i] = self.sv.cnt
            self.store_xc(t["xbuf"][:], c)

    def emit_all(self):
        t = self.t
        self.emit_loads()
        self.emit_prologue()
        self.emit_ffn(t["x"][:], t["xbuf"][:])
        self.emit_attn()
        self.emit_ffn(t["xbuf"][:], t["o"][:])
        nc = self.nc
        for s in self.sst:
            nc.sync.wait_ge(s.h, s.cnt)


def build_program(n_tok=4096):
    nc = bass.Bass("TRN2", target_bir_lowering=False, debug=False)
    b = Builder(nc, n_tok)
    b.emit_all()
    return nc


def prep_inputs(L0_attn_q_proj, L0_attn_k_proj, L0_ffn_P_w, L0_ffn_U1, L0_ffn_U2,
                L0_ffn_U3, L0_ffn_W1, L0_ffn_W2, L0_ffn_b2):
    c = np.ascontiguousarray
    f = np.float32
    return {
        "w1t": c(np.asarray(L0_ffn_W1, dtype=f).transpose(0, 2, 1)),
        "w2t": c(np.asarray(L0_ffn_W2, dtype=f).transpose(0, 2, 1)),
        "pwt": c(np.asarray(L0_ffn_P_w, dtype=f).T),
        "ut": c(np.concatenate([np.asarray(L0_ffn_U1, dtype=f).T,
                                np.asarray(L0_ffn_U2, dtype=f).T,
                                np.asarray(L0_ffn_U3, dtype=f).T], axis=1)),
        "b2": c(np.asarray(L0_ffn_b2, dtype=f)),
        "qp": c(np.asarray(L0_attn_q_proj, dtype=f)),
        "kp": c(np.asarray(L0_attn_k_proj, dtype=f)),
        "ident": np.eye(128, dtype=f),
    }


def kernel(L0_attn_x, L0_attn_q_proj, L0_attn_k_proj, L0_ffn_P_w, L0_ffn_U1,
           L0_ffn_U2, L0_ffn_U3, L0_ffn_W1, L0_ffn_W2, L0_ffn_b2):
    x = np.ascontiguousarray(np.asarray(L0_attn_x, dtype=np.float32))
    B = x.shape[0]
    shared = prep_inputs(L0_attn_q_proj, L0_attn_k_proj, L0_ffn_P_w, L0_ffn_U1,
                         L0_ffn_U2, L0_ffn_U3, L0_ffn_W1, L0_ffn_W2, L0_ffn_b2)
    nc = build_program(n_tok=x.shape[1])
    in_maps = [dict(shared, x=np.ascontiguousarray(x[c])) for c in range(B)]
    res = run_bass_kernel_spmd(nc, in_maps, core_ids=list(range(B)))
    return np.stack([res.results[c]["o"] for c in range(B)], axis=0)


# version 15
# speedup vs baseline: 1.0468x; 1.0468x over previous
import numpy as np
import concourse.bass as bass
import concourse.mybir as mybir
from concourse.bass_utils import run_bass_kernel_spmd

F32 = mybir.dt.float32
BF16 = mybir.dt.bfloat16
AF = mybir.ActivationFunctionType
ALU = mybir.AluOpType
AX = mybir.AxisListType

D = 1024
KC = D // 128
NE = 12
DA = 64
PD = 256
TPC = 8
TW = TPC * 128
BIG = 1.0e30
INV_TAU = 1.0 / (1.0 + 1e-8)
LN_EPS = 1e-5
Z_EPS = 1e-8


class Sem:
    def __init__(self, nc, name):
        self.h = nc.semaphore(name).__enter__()
        self.cnt = 0

    def inc(self, instr, n=1):
        instr.then_inc(self.h, n)
        self.cnt += n
        return instr


class Builder:

    def __init__(self, nc, n_tok):
        self.nc = nc
        self.n_tok = n_tok
        self.n_tiles = n_tok // 128
        self.n_chunks = self.n_tiles // TPC
        t = {}
        self.t = t
        t["x"] = nc.dram_tensor("x", [n_tok, D], F32, kind="ExternalInput")
        t["o"] = nc.dram_tensor("o", [n_tok, D], F32, kind="ExternalOutput")
        t["xbuf"] = nc.dram_tensor("xbuf", [n_tok, D], F32, kind="Internal")
        t["w1t"] = nc.dram_tensor("w1t", [NE, D, D], F32, kind="ExternalInput")
        t["w2t"] = nc.dram_tensor("w2t", [NE, D, D], F32, kind="ExternalInput")
        t["pwt"] = nc.dram_tensor("pwt", [D, DA], F32, kind="ExternalInput")
        t["ut"] = nc.dram_tensor("ut", [DA, 3 * NE], F32, kind="ExternalInput")
        t["b2"] = nc.dram_tensor("b2", [NE, D], F32, kind="ExternalInput")
        t["qp"] = nc.dram_tensor("qp", [D, PD], F32, kind="ExternalInput")
        t["kp"] = nc.dram_tensor("kp", [D, PD], F32, kind="ExternalInput")
        t["ident"] = nc.dram_tensor("ident", [128, 128], F32, kind="ExternalInput")
        sb = nc.alloc_sbuf_tensor
        self.xc = [sb("xc0", [128, TPC * D], F32), sb("xc1", [128, TPC * D], F32)]
        self.A = sb("A", [128, TPC * D], F32)
        self.Bhi = sb("Bhi", [128, KC * TW], BF16)
        self.Blo = sb("Blo", [128, KC * TW], BF16)
        self.wb = [sb("wb0", [128, 2048], F32), sb("wb1", [128, 2048], F32)]
        self.att = sb("att", [128, 2 * TW], F32)
        self.kvs = sb("kvs", [128, 2 * D], F32)
        self.ident_s = sb("ident_s", [128, 128], F32)
        self.pwhi = sb("pwhi", [128, KC * DA], BF16)
        self.pwlo = sb("pwlo", [128, KC * DA], BF16)
        self.ut_s = sb("ut_s", [DA, 3 * NE], F32)
        self.b2_s = sb("b2_s", [NE, D], F32)
        self.qphi = sb("qphi", [128, KC * PD], BF16)
        self.qplo = sb("qplo", [128, KC * PD], BF16)
        self.kphi = sb("kphi", [128, KC * PD], BF16)
        self.kplo = sb("kplo", [128, KC * PD], BF16)
        self.C1 = sb("C1", [128, TPC * NE], F32)
        self.C2 = sb("C2", [128, TPC * NE], F32)
        self.C3 = sb("C3", [128, TPC * NE], F32)
        self.C3T = sb("C3T", [NE, TPC * 128], F32)
        self.xa_s = sb("xa_s", [DA, TW], F32)
        self.z_s = sb("z_s", [128, TPC * 3 * NE], F32)
        self.mcol = sb("mcol", [128, self.n_tiles], F32)
        self.rcol = sb("rcol", [128, self.n_tiles], F32)
        self.zcol = sb("zcol", [128, TPC], F32)
        self.scr = sb("scr", [128, 16], F32)
        self.g12 = sb("g12", [128, 4 * TPC * 3], F32)
        self.e144 = sb("e144", [128, TPC * 3 * NE], F32)
        self.zt144 = sb("zt144", [128, TPC * 3 * NE], F32)
        self.ones = sb("ones", [128, 1], F32)
        self.gtmp = sb("gtmp", [128, 4 * 128], F32)
        self.pA = nc.alloc_psum_tensor("pA", [128, 2048], F32)
        self.pB = nc.alloc_psum_tensor("pB", [128, 2048], F32)
        self.sv = Sem(nc, "sv")
        self.ss = Sem(nc, "ss")
        self.st = Sem(nc, "st_t")
        self.sw = [Sem(nc, "sw0"), Sem(nc, "sw1")]
        self.sin = Sem(nc, "sin")
        self.sxc = [Sem(nc, "sxc0"), Sem(nc, "sxc1")]
        self.sst = [Sem(nc, "sst0"), Sem(nc, "sst1")]
        self.wb_free_at = [0, 0]
        self.xc_free = [[], []]
        self.xc_ready = [0, 0]

    def wv(self, eng, val=None):
        eng.wait_ge(self.sv.h, self.sv.cnt if val is None else val)

    def ws(self, eng, val=None):
        eng.wait_ge(self.ss.h, self.ss.cnt if val is None else val)

    def wt(self, eng, val=None):
        eng.wait_ge(self.st.h, self.st.cnt if val is None else val)

    def load_xc(self, src_dram, chunk):
        nc = self.nc
        p = chunk % 2
        for sem, val in self.xc_free[p]:
            nc.sync.wait_ge(sem, val)
        self.xc_free[p] = []
        rows = src_dram[chunk * TW:(chunk + 1) * TW, :]
        src = rows.rearrange("(t p) f -> p t f", p=128)
        dst = self.xc[p][:].rearrange("p (t f) -> p t f", f=D)
        self.sxc[p].inc(nc.sync.dma_start(out=dst, in_=src), 16)
        self.xc_ready[p] = self.sxc[p].cnt

    def store_xc(self, dst_dram, chunk):
        nc = self.nc
        p = chunk % 2
        nc.sync.wait_ge(self.sv.h, self.sv.cnt)
        rows = dst_dram[chunk * TW:(chunk + 1) * TW, :]
        dst = rows.rearrange("(t p) f -> p t f", p=128)
        src = self.xc[p][:].rearrange("p (t f) -> p t f", f=D)
        self.sst[p].inc(nc.sync.dma_start(out=dst, in_=src), 16)
        self.xc_free[p] = [(self.sst[p].h, self.sst[p].cnt)]

    def emit_loads(self):
        nc, t = self.nc, self.t
        self.sin.inc(nc.sync.dma_start(out=self.ident_s[:], in_=t["ident"][:]), 16)
        self.sin.inc(nc.sync.dma_start(out=self.ut_s[:], in_=t["ut"][:]), 16)
        self.sin.inc(nc.sync.dma_start(out=self.b2_s[:], in_=t["b2"][:]), 16)
        for eng in (nc.vector, nc.scalar, nc.tensor):
            eng.wait_ge(self.sin.h, self.sin.cnt)
        self.sv.inc(nc.vector.memset(self.ones[:], 1.0))

    def emit_prologue(self):
        nc, t = self.nc, self.t
        V, S = nc.vector, nc.scalar
        for name, hi, lo, cols, j in (("qp", self.qphi, self.qplo, KC * PD, PD),
                                      ("kp", self.kphi, self.kplo, KC * PD, PD),
                                      ("pwt", self.pwhi, self.pwlo, KC * DA, DA)):
            stage = self.A[:, 0:cols]
            src = t[name][:].rearrange("(kc p) j -> p kc j", p=128)
            nc.sync.wait_ge(self.sv.h, self.sv.cnt)
            nc.sync.wait_ge(self.ss.h, self.ss.cnt)
            self.sin.inc(nc.sync.dma_start(
                out=stage.rearrange("p (kc j) -> p kc j", j=j), in_=src), 16)
            S.wait_ge(self.sin.h, self.sin.cnt)
            self.ss.inc(S.copy(hi[:], stage))
            V.wait_ge(self.sin.h, self.sin.cnt)
            self.ws(V)
            self.sv.inc(V.tensor_sub(lo[:], stage, hi[:]))

    def emit_ln(self, chunk, save_stats=False, use_saved=False, a_free_t=None):
        nc = self.nc
        V, S = nc.vector, nc.scalar
        p = chunk % 2
        xv = self.xc[p]
        for tt in range(TPC):
            g = chunk * TPC + tt
            xt = xv[:, tt * D:(tt + 1) * D]
            At = self.A[:, tt * D:(tt + 1) * D]
            if use_saved:
                m = self.mcol[:, g:g + 1]
                r = self.rcol[:, g:g + 1]
                if tt == 0:
                    V.wait_ge(self.sxc[p].h, self.xc_ready[p])
                    self.wt(V, a_free_t)
                    self.ws(V)
                    self.wv(V)
                self.sv.inc(V.tensor_scalar(At, xt, m, r, ALU.subtract, ALU.mult))
                continue
            m = self.mcol[:, g:g + 1] if save_stats else self.scr[:, 0:1]
            r = self.rcol[:, g:g + 1] if save_stats else self.scr[:, 1:2]
            sumc = self.scr[:, 2:3]
            sqc = self.scr[:, 3:4]
            varc = self.scr[:, 4:5]
            msq = self.scr[:, 5:6]
            if tt == 0:
                V.wait_ge(self.sxc[p].h, self.xc_ready[p])
                self.wv(V)
            self.sv.inc(V.reduce_sum(sumc, xt, axis=AX.X))
            if tt == 0:
                S.wait_ge(self.sxc[p].h, self.xc_ready[p])
                self.wt(S, a_free_t)
            self.wv(S)
            self.ss.inc(S.activation(At, xt, AF.Square, accum_out=sqc))
            self.ws(V)
            self.wv(V)
            self.sv.inc(V.tensor_scalar_mul(m, sumc, 1.0 / D))
            self.wv(V)
            self.sv.inc(V.tensor_mul(msq, m, m))
            self.sv.inc(V.tensor_scalar(varc, sqc, 1.0 / D, LN_EPS, ALU.mult, ALU.add))
            self.wv(V)
            self.sv.inc(V.tensor_sub(varc, varc, msq))
            self.wv(S)
            self.ss.inc(S.activation(varc, varc, AF.Ln))
            self.ws(S)
            self.ss.inc(S.activation(r, varc, AF.Exp, scale=-0.5))
            self.ws(V)
            self.sv.inc(V.tensor_scalar(At, xt, m, r, ALU.subtract, ALU.mult))

    def emit_transpose_A_to_B(self, arena, slots):
        nc = self.nc
        T, S, V = nc.tensor, nc.scalar, nc.vector
        slot_free = [None] * len(slots)
        first = True
        for tt in range(TPC):
            for kc in range(KC):
                i = tt * KC + kc
                sl = i % len(slots)
                base = slots[sl]
                pview = arena[:, base: base + 128]
                src = self.A[:, tt * D + kc * 128: tt * D + (kc + 1) * 128]
                if first:
                    self.wv(T)
                    self.ws(T)
                    first = False
                elif slot_free[sl] is not None:
                    self.ws(T, slot_free[sl][0])
                    self.wv(T, slot_free[sl][1])
                self.st.inc(T.transpose(pview, src, self.ident_s[:]))
                self.wt(S)
                hi = self.Bhi[:, kc * TW + tt * 128: kc * TW + (tt + 1) * 128]
                lo = self.Blo[:, kc * TW + tt * 128: kc * TW + (tt + 1) * 128]
                self.ss.inc(S.copy(hi, pview))
                self.wt(V)
                self.ws(V)
                self.sv.inc(V.tensor_sub(lo, pview, hi))
                slot_free[sl] = (self.ss.cnt, self.sv.cnt)

    def emit_transpose_gelu_A_to_B(self, arena, slots):
        nc = self.nc
        T, S, V = nc.tensor, nc.scalar, nc.vector
        slot_free = [None] * len(slots)
        first = True
        for tt in range(TPC):
            for kc in range(KC):
                i = tt * KC + kc
                sl = i % len(slots)
                base = slots[sl]
                pview = arena[:, base: base + 128]
                gt = self.gtmp[:, sl * 128:(sl + 1) * 128]
                src = self.A[:, tt * D + kc * 128: tt * D + (kc + 1) * 128]
                if first:
                    self.wv(T)
                    self.ws(T)
                    first = False
                elif slot_free[sl] is not None:
                    self.ws(T, slot_free[sl][0])
                    self.wv(T, slot_free[sl][1])
                self.st.inc(T.transpose(pview, src, self.ident_s[:]))
                self.wt(S)
                hi = self.Bhi[:, kc * TW + tt * 128: kc * TW + (tt + 1) * 128]
                lo = self.Blo[:, kc * TW + tt * 128: kc * TW + (tt + 1) * 128]
                self.ss.inc(S.activation(gt, pview, AF.Gelu))
                self.ws(S)
                self.ss.inc(S.copy(hi, gt))
                self.ws(V)
                self.sv.inc(V.tensor_sub(lo, gt, hi))
                slot_free[sl] = (self.ss.cnt, self.sv.cnt)

    def _mm3(self, T, out, lhi, llo, rhi, rlo, start, stop):
        T.matmul(out, lhi, rhi, start=start, stop=False)
        T.matmul(out, lhi, rlo, start=False, stop=False)
        return T.matmul(out, llo, rhi, start=False, stop=stop)

    def emit_routing(self, chunk):
        nc = self.nc
        T, V, S = nc.tensor, nc.vector, nc.scalar
        self.ws(T)
        self.wv(T)
        ins = None
        for kc in range(KC):
            for th in range(2):
                xap = self.pB[0:DA, th * 512:(th + 1) * 512]
                ins = self._mm3(T, xap,
                                self.pwhi[:, kc * DA:(kc + 1) * DA],
                                self.pwlo[:, kc * DA:(kc + 1) * DA],
                                self.Bhi[:, kc * TW + th * 512: kc * TW + (th + 1) * 512],
                                self.Blo[:, kc * TW + th * 512: kc * TW + (th + 1) * 512],
                                start=(kc == 0), stop=(kc == KC - 1))
        self.st.inc(ins)
        self.wt(V)
        self.sv.inc(V.tensor_copy(self.xa_s[:], self.pB[0:DA, 0:TW]))
        zp = self.pB[:, 1024:1024 + TPC * 3 * NE]
        self.wv(T)
        for tt in range(TPC):
            for r in range(3):
                self.st.inc(T.matmul(zp[:, tt * 36 + r * NE: tt * 36 + (r + 1) * NE],
                                     self.xa_s[:, tt * 128:(tt + 1) * 128],
                                     self.ut_s[:, r * NE:(r + 1) * NE],
                                     start=True, stop=True))
        self.wt(V)
        self.sv.inc(V.tensor_copy(self.z_s[:], zp))
        NG = TPC * 3
        H = NG // 2
        vdone = [self.sv.cnt, self.sv.cnt]
        sdone = [self.ss.cnt, self.ss.cnt]

        def half(ap3, h):
            return ap3[:, h * H:(h + 1) * H, :]

        z3 = self.z_s[:].rearrange("p (g w) -> p g w", w=NE)
        e3 = self.e144[:].rearrange("p (g w) -> p g w", w=NE)
        zt3 = self.zt144[:].rearrange("p (g w) -> p g w", w=NE)
        M1 = self.g12[:, 0:NG]
        M3 = self.g12[:, NG:2 * NG]
        DEN = self.g12[:, 2 * NG:3 * NG]
        RD = self.g12[:, 3 * NG:4 * NG]
        M1b = M1.unsqueeze(2).broadcast_to([128, NG, NE])
        M3b = M3.unsqueeze(2).broadcast_to([128, NG, NE])

        def vstep(fn):
            for h in (0, 1):
                self.wv(V, vdone[h])
                self.ws(V, sdone[h])
                self.sv.inc(fn(h))
                vdone[h] = self.sv.cnt

        def hs(col):
            return lambda h: col[:, h * H:(h + 1) * H]

        vstep(lambda h: V.reduce_max(hs(M1)(h), half(z3, h), axis=AX.X))
        vstep(lambda h: V.tensor_tensor(half(e3, h), half(z3, h), half(M1b, h), ALU.is_ge))
        vstep(lambda h: V.scalar_tensor_tensor(half(zt3, h), half(e3, h), -BIG,
                                               half(z3, h), ALU.mult, ALU.add))
        vstep(lambda h: V.reduce_max(hs(M3)(h), half(zt3, h), axis=AX.X))
        vstep(lambda h: V.tensor_tensor(half(e3, h), half(zt3, h), half(M3b, h), ALU.is_ge))
        vstep(lambda h: V.scalar_tensor_tensor(half(zt3, h), half(e3, h), -BIG,
                                               half(zt3, h), ALU.mult, ALU.add))
        vstep(lambda h: V.reduce_max(hs(M3)(h), half(zt3, h), axis=AX.X))
        vstep(lambda h: V.tensor_tensor(half(zt3, h), half(z3, h), half(M1b, h), ALU.subtract))
        vstep(lambda h: V.tensor_tensor(half(e3, h), half(z3, h), half(M3b, h), ALU.is_ge))
        for h in (0, 1):
            S.wait_ge(self.sv.h, vdone[h])
            zt_h = self.zt144[:, h * H * NE:(h + 1) * H * NE]
            self.ss.inc(S.activation(zt_h, zt_h, AF.Exp, scale=INV_TAU))
            sdone[h] = self.ss.cnt
        vstep(lambda h: V.tensor_mul(self.zt144[:, h * H * NE:(h + 1) * H * NE],
                                     self.zt144[:, h * H * NE:(h + 1) * H * NE],
                                     self.e144[:, h * H * NE:(h + 1) * H * NE]))
        vstep(lambda h: V.reduce_sum(hs(DEN)(h), half(zt3, h), axis=AX.X))
        vstep(lambda h: V.reciprocal(hs(RD)(h), hs(DEN)(h)))
        vstep(lambda h: V.tensor_scalar_mul(hs(DEN)(h), hs(RD)(h), 0.5))
        for h in (0, 1):
            self.wv(V, vdone[h])
        zt4 = self.zt144[:].rearrange("p (t r w) -> p t r w", r=3, w=NE)
        for r in range(3):
            C = (self.C1, self.C2, self.C3)[r]
            sc = RD if r == 0 else DEN
            sc_r = sc.rearrange("p (t r) -> p t r", r=3)[:, :, r]
            sc_rb = sc_r.unsqueeze(2).broadcast_to([128, TPC, NE])
            self.sv.inc(V.tensor_tensor(
                C[:].rearrange("p (t w) -> p t w", w=NE), zt4[:, :, r, :], sc_rb,
                ALU.mult))

    def load_quarter(self, wdram, pair, q):
        nc = self.nc
        S, V = nc.scalar, nc.vector
        e, oh = pair // 2, pair % 2
        kb, half = q // 2, q % 2
        gq = pair * 4 + q
        slot = gq % 2
        r0 = kb * 512 + half * 256
        src = wdram[e, r0:r0 + 256, oh * 512:(oh + 1) * 512]
        src = src.rearrange("(k p) o -> p k o", p=128)
        f32 = self.wb[slot][:, 0:1024]
        wbB = self.wb[slot][:].bitcast(BF16)
        hi = wbB[:, 2048:3072]
        lo = wbB[:, 3072:4096]
        nc.sync.wait_ge(self.st.h, self.wb_free_at[slot])
        self.sw[slot].inc(nc.sync.dma_start(
            out=f32.rearrange("p (k o) -> p k o", o=512), in_=src), 16)
        S.wait_ge(self.sw[slot].h, self.sw[slot].cnt)
        self.wt(S, self.wb_free_at[slot])
        self.ss.inc(S.copy(hi, f32))
        V.wait_ge(self.sw[slot].h, self.sw[slot].cnt)
        self.ws(V)
        self.wt(V, self.wb_free_at[slot])
        self.sv.inc(V.tensor_sub(lo, f32, hi))
        self.split_done[slot] = (self.ss.cnt, self.sv.cnt)

    def preload_mixture(self, wdram):
        self.split_done = [None, None]
        self.load_quarter(wdram, 0, 0)
        self.load_quarter(wdram, 0, 1)

    def emit_mixture(self, chunk, wdram, C, dest_is_xc, first_clears,
                     preloaded=False, hooks=None):
        nc = self.nc
        T, V = nc.tensor, nc.vector
        xv = self.xc[chunk % 2]

        self.wv(T)
        self.ws(T)
        if not preloaded:
            self.preload_mixture(wdram)
        evac_after = {}
        for pair in range(2 * NE):
            e, oh = pair // 2, pair % 2
            for q in range(4):
                gq = pair * 4 + q
                slot = gq % 2
                kb, half = q // 2, q % 2
                wbB = self.wb[slot][:].bitcast(BF16)
                sd = self.split_done[slot]
                self.ws(T, sd[0])
                self.wv(T, sd[1])
                for th in range(2):
                    grp = self.pA if th == 0 else self.pB
                    if q == 0 and pair >= 1 and (pair - 1, th) in evac_after:
                        T.wait_ge(self.sv.h, evac_after[(pair - 1, th)])
                    last = None
                    for k4h in range(2):
                        kc = kb * 4 + half * 2 + k4h
                        whi = wbB[:, 2048 + k4h * 512: 2048 + (k4h + 1) * 512]
                        wlo = wbB[:, 3072 + k4h * 512: 3072 + (k4h + 1) * 512]
                        for t4 in range(4):
                            tt = th * 4 + t4
                            bhi = self.Bhi[:, kc * TW + tt * 128: kc * TW + (tt + 1) * 128]
                            blo = self.Blo[:, kc * TW + tt * 128: kc * TW + (tt + 1) * 128]
                            last = self._mm3(T, grp[:, t4 * 512:(t4 + 1) * 512],
                                             bhi, blo, whi, wlo,
                                             start=(kc == 0), stop=(kc == KC - 1))
                    self.st.inc(last)
                    if q == 3:
                        self.wt(V)
                        lastv = None
                        for t4 in range(4):
                            tt = th * 4 + t4
                            ps = grp[:, t4 * 512:(t4 + 1) * 512]
                            if dest_is_xc:
                                dst = xv[:, tt * D + oh * 512: tt * D + oh * 512 + 512]
                            else:
                                dst = self.A[:, tt * D + oh * 512: tt * D + oh * 512 + 512]
                            col = C[:, tt * NE + e: tt * NE + e + 1]
                            if first_clears and e == 0:
                                lastv = V.tensor_scalar(dst, ps, col, None, ALU.mult)
                            else:
                                lastv = V.scalar_tensor_tensor(dst, ps, col, dst,
                                                               ALU.mult, ALU.add)
                        self.sv.inc(lastv)
                        evac_after[(pair, th)] = self.sv.cnt
                self.wb_free_at[slot] = self.st.cnt
                if gq + 2 < 2 * NE * 4:
                    npair, nq = (gq + 2) // 4, (gq + 2) % 4
                    self.load_quarter(wdram, npair, nq)
            if hooks and pair in hooks:
                hooks[pair]()

    def emit_b2_term(self, chunk):
        nc = self.nc
        T, V, S = nc.tensor, nc.vector, nc.scalar
        xv = self.xc[chunk % 2]
        self.wv(T)
        self.ws(T)
        for tt in range(TPC):
            b = 1024 if tt % 2 == 0 else 1536
            pv = self.pB[0:NE, b: b + 128]
            if tt >= 2:
                self.ws(T)
            self.st.inc(T.transpose(pv, self.C3[:, tt * NE:(tt + 1) * NE], self.ident_s[:]))
            self.wt(S)
            self.ss.inc(S.copy(self.C3T[:, tt * 128:(tt + 1) * 128], pv))
        self.ws(T)
        add_done = {}
        for tt in range(TPC):
            for dh in range(2):
                i = tt * 2 + dh
                pv = self.pA[:, (i % 4) * 512: (i % 4) * 512 + 512]
                if i >= 4:
                    self.wv(T, add_done[i - 4])
                self.st.inc(T.matmul(pv, self.C3T[:, tt * 128:(tt + 1) * 128],
                                     self.b2_s[:, dh * 512:(dh + 1) * 512],
                                     start=True, stop=True))
                self.wt(V)
                dst = xv[:, tt * D + dh * 512: tt * D + dh * 512 + 512]
                self.sv.inc(V.tensor_add(dst, dst, pv))
                add_done[i] = self.sv.cnt

    def emit_ffn_chunk(self, chunk, src_dram, dst_dram, ln_done=False):
        if chunk + 1 < self.n_chunks:
            self.load_xc(src_dram, chunk + 1)
        if not ln_done:
            self.emit_ln(chunk)
        self.preload_mixture(self.t["w1t"][:])
        self.emit_transpose_A_to_B(self.pA, [0, 512, 1024, 1536])
        self.emit_routing(chunk)
        self.emit_mixture(chunk, self.t["w1t"][:], self.C1,
                          dest_is_xc=False, first_clears=True, preloaded=True)
        self.preload_mixture(self.t["w2t"][:])
        self.emit_transpose_gelu_A_to_B(self.pA, [0, 512, 1024, 1536])
        self.emit_b2_term(chunk)
        hooks = None
        if chunk + 1 < self.n_chunks:
            hooks = {2: lambda: self.emit_ln(chunk + 1)}
        self.emit_mixture(chunk, self.t["w2t"][:], self.C2,
                          dest_is_xc=True, first_clears=False, preloaded=True,
                          hooks=hooks)
        self.store_xc(dst_dram, chunk)

    def emit_ffn(self, src_dram, dst_dram, first_ready=False):
        if not first_ready:
            self.load_xc(src_dram, 0)
        for c in range(self.n_chunks):
            self.emit_ffn_chunk(c, src_dram, dst_dram,
                                ln_done=(c > 0 or first_ready))

    def emit_attn(self):
        nc = self.nc
        T, V, S = nc.tensor, nc.vector, nc.scalar
        t = self.t
        kprime = self.att[:, 0:TPC * PD]
        kv_s0 = self.kvs[:, 0:D]
        kv_s1 = self.kvs[:, D:2 * D]
        ccol = self.scr[:, 10:12]
        self.wv(nc.vector)
        self.sv.inc(V.memset(ccol, 0.0))
        self.load_xc(t["xbuf"][:], 0)
        for c in range(self.n_chunks):
            p = c % 2
            xv = self.xc[p]
            if c + 1 < self.n_chunks:
                self.load_xc(t["xbuf"][:], c + 1)
            if c == 0:
                self.emit_ln(c, save_stats=True)
            self.emit_transpose_A_to_B(self.pB, [0, 512, 1024, 1536])
            a_free = self.st.cnt
            self._attn_afree = a_free
            self.ws(T)
            self.wv(T)
            kp_done = {}
            for th in range(2):
                last = None
                for t4 in range(4):
                    tt = th * 4 + t4
                    pv = self.pB[:, t4 * PD:(t4 + 1) * PD]
                    if th == 1:
                        self.ws(T, kp_done[0])
                    for kc in range(KC):
                        last = self._mm3(
                            T, pv,
                            self.Bhi[:, kc * TW + tt * 128: kc * TW + (tt + 1) * 128],
                            self.Blo[:, kc * TW + tt * 128: kc * TW + (tt + 1) * 128],
                            self.kphi[:, kc * PD:(kc + 1) * PD],
                            self.kplo[:, kc * PD:(kc + 1) * PD],
                            start=(kc == 0), stop=(kc == KC - 1))
                    self.st.inc(last)
                self.wt(S)
                for t4 in range(4):
                    tt = th * 4 + t4
                    self.ss.inc(S.activation(kprime[:, tt * PD:(tt + 1) * PD],
                                             self.pB[:, t4 * PD:(t4 + 1) * PD], AF.Exp))
                kp_done[th] = self.ss.cnt
            self.ws(V)
            lastv = None
            for tt in range(TPC):
                g = c * TPC + tt
                lastv = V.tensor_scalar(kprime[:, tt * PD:(tt + 1) * PD],
                                        kprime[:, tt * PD:(tt + 1) * PD],
                                        self.rcol[:, g:g + 1], None, ALU.mult)
            self.sv.inc(lastv)
            self.ws(T)
            self.wv(T)
            last = None
            for ec in range(2):
                for dh in range(2):
                    pv = self.pA[:, (ec * 2 + dh) * 512:(ec * 2 + dh + 1) * 512]
                    for tt in range(TPC):
                        last = T.matmul(pv,
                                        kprime[:, tt * PD + ec * 128: tt * PD + ec * 128 + 128],
                                        xv[:, tt * D + dh * 512: tt * D + dh * 512 + 512],
                                        start=(c == 0 and tt == 0),
                                        stop=(c == self.n_chunks - 1 and tt == TPC - 1))
            self.st.inc(last)
            last = None
            for ec in range(2):
                cps = self.pB[:, ec * 2: ec * 2 + 1]
                for tt in range(TPC):
                    g = c * TPC + tt
                    last = T.matmul(cps,
                                    kprime[:, tt * PD + ec * 128: tt * PD + ec * 128 + 128],
                                    self.mcol[:, g:g + 1],
                                    start=(tt == 0), stop=(tt == TPC - 1))
            self.st.inc(last)
            kv_read_done = self.st.cnt
            if c + 1 < self.n_chunks:
                self.emit_ln(c + 1, save_stats=True, a_free_t=a_free)
            self.wt(V)
            self.sv.inc(V.tensor_add(ccol[:, 0:1], ccol[:, 0:1], self.pB[:, 0:1]))
            self.sv.inc(V.tensor_add(ccol[:, 1:2], ccol[:, 1:2], self.pB[:, 2:3]))
            self.wv(V)
            self.xc_free[p] = [(self.st.h, kv_read_done)]
        self.load_xc(t["xbuf"][:], 0)
        self.emit_ln(0, use_saved=True, a_free_t=self._attn_afree)
        self.wt(V)
        self.sv.inc(V.tensor_copy(kv_s0, self.pA[:, 0:1024]))
        self.sv.inc(V.tensor_copy(kv_s1, self.pA[:, 1024:2048]))
        self.wv(V)
        self.sv.inc(V.tensor_scalar(kv_s0, kv_s0, ccol[:, 0:1], None, ALU.subtract))
        self.sv.inc(V.tensor_scalar(kv_s1, kv_s1, ccol[:, 1:2], None, ALU.subtract))
        for c in range(self.n_chunks):
            p = c % 2
            xv = self.xc[p]
            if c + 1 < self.n_chunks:
                self.load_xc(t["xbuf"][:], c + 1)
            self.emit_transpose_A_to_B(self.pA, [0, 512, 1024, 1536])
            self._attn_afree = self.st.cnt
            self.ws(T)
            self.wv(T)
            q_done = {}
            for ec in range(2):
                for th in range(2):
                    i = ec * 2 + th
                    pv = self.pB[:, (i % 2) * 512:(i % 2) * 512 + 512]
                    if i >= 2:
                        self.ws(T, q_done[i - 2])
                    last = None
                    for kc in range(KC):
                        last = self._mm3(
                            T, pv,
                            self.qphi[:, kc * PD + ec * 128: kc * PD + ec * 128 + 128],
                            self.qplo[:, kc * PD + ec * 128: kc * PD + ec * 128 + 128],
                            self.Bhi[:, kc * TW + th * 512: kc * TW + (th + 1) * 512],
                            self.Blo[:, kc * TW + th * 512: kc * TW + (th + 1) * 512],
                            start=(kc == 0), stop=(kc == KC - 1))
                    self.st.inc(last)
                    self.wt(S)
                    self.ss.inc(S.activation(
                        self.att[:, ec * TW + th * 512: ec * TW + (th + 1) * 512],
                        pv, AF.Exp))
                    q_done[i] = self.ss.cnt
            self.ws(T)
            last = None
            for tt in range(TPC):
                pv = self.pB[:, 1024 + tt * 8: 1024 + tt * 8 + 1]
                for ec in range(2):
                    last = T.matmul(pv,
                                    self.att[:, ec * TW + tt * 128: ec * TW + (tt + 1) * 128],
                                    self.ones[:], start=(ec == 0), stop=(ec == 1))
            self.st.inc(last)
            self.wt(V)
            for tt in range(TPC):
                zc = self.zcol[:, tt:tt + 1]
                self.sv.inc(V.tensor_scalar_add(
                    zc, self.pB[:, 1024 + tt * 8: 1024 + tt * 8 + 1], Z_EPS))
            self.wv(V)
            for tt in range(TPC):
                zc = self.zcol[:, tt:tt + 1]
                self.sv.inc(V.reciprocal(zc, zc))
            if c + 1 < self.n_chunks:
                self.emit_ln(c + 1, use_saved=True)
            else:
                self.load_xc(t["xbuf"][:], 0)
                self.emit_ln(0, a_free_t=self._attn_afree)
            num_done = {}
            for tt in range(TPC):
                for dh in range(2):
                    i = tt * 2 + dh
                    base = 1536 if i % 2 == 0 else 1024
                    pv = self.pB[:, base: base + 512]
                    if i < 2:
                        self.wv(T)
                    else:
                        self.wv(T, num_done[i - 2])
                    for ec in range(2):
                        last = T.matmul(pv,
                                        self.att[:, ec * TW + tt * 128: ec * TW + (tt + 1) * 128],
                                        (kv_s0 if ec == 0 else kv_s1)[:, dh * 512:(dh + 1) * 512],
                                        start=(ec == 0), stop=(ec == 1))
                    self.st.inc(last)
                    self.wt(V)
                    dst = xv[:, tt * D + dh * 512: tt * D + dh * 512 + 512]
                    self.sv.inc(V.scalar_tensor_tensor(dst, pv, self.zcol[:, tt:tt + 1],
                                                       dst, ALU.mult, ALU.add))
                    num_done[i] = self.sv.cnt
            self.store_xc(t["xbuf"][:], c)

    def emit_all(self):
        t = self.t
        self.emit_loads()
        self.emit_prologue()
        self.emit_ffn(t["x"][:], t["xbuf"][:])
        self.emit_attn()
        self.emit_ffn(t["xbuf"][:], t["o"][:], first_ready=True)
        nc = self.nc
        for s in self.sst:
            nc.sync.wait_ge(s.h, s.cnt)


def build_program(n_tok=4096):
    nc = bass.Bass("TRN2", target_bir_lowering=False, debug=False)
    b = Builder(nc, n_tok)
    b.emit_all()
    return nc


def prep_inputs(L0_attn_q_proj, L0_attn_k_proj, L0_ffn_P_w, L0_ffn_U1, L0_ffn_U2,
                L0_ffn_U3, L0_ffn_W1, L0_ffn_W2, L0_ffn_b2):
    c = np.ascontiguousarray
    f = np.float32
    return {
        "w1t": c(np.asarray(L0_ffn_W1, dtype=f).transpose(0, 2, 1)),
        "w2t": c(np.asarray(L0_ffn_W2, dtype=f).transpose(0, 2, 1)),
        "pwt": c(np.asarray(L0_ffn_P_w, dtype=f).T),
        "ut": c(np.concatenate([np.asarray(L0_ffn_U1, dtype=f).T,
                                np.asarray(L0_ffn_U2, dtype=f).T,
                                np.asarray(L0_ffn_U3, dtype=f).T], axis=1)),
        "b2": c(np.asarray(L0_ffn_b2, dtype=f)),
        "qp": c(np.asarray(L0_attn_q_proj, dtype=f)),
        "kp": c(np.asarray(L0_attn_k_proj, dtype=f)),
        "ident": np.eye(128, dtype=f),
    }


def kernel(L0_attn_x, L0_attn_q_proj, L0_attn_k_proj, L0_ffn_P_w, L0_ffn_U1,
           L0_ffn_U2, L0_ffn_U3, L0_ffn_W1, L0_ffn_W2, L0_ffn_b2):
    x = np.ascontiguousarray(np.asarray(L0_attn_x, dtype=np.float32))
    B = x.shape[0]
    shared = prep_inputs(L0_attn_q_proj, L0_attn_k_proj, L0_ffn_P_w, L0_ffn_U1,
                         L0_ffn_U2, L0_ffn_U3, L0_ffn_W1, L0_ffn_W2, L0_ffn_b2)
    nc = build_program(n_tok=x.shape[1])
    in_maps = [dict(shared, x=np.ascontiguousarray(x[c])) for c in range(B)]
    res = run_bass_kernel_spmd(nc, in_maps, core_ids=list(range(B)))
    return np.stack([res.results[c]["o"] for c in range(B)], axis=0)


# version 16
# speedup vs baseline: 1.0874x; 1.0388x over previous
import numpy as np
import concourse.bass as bass
import concourse.mybir as mybir
from concourse.bass_utils import run_bass_kernel_spmd

F32 = mybir.dt.float32
BF16 = mybir.dt.bfloat16
AF = mybir.ActivationFunctionType
ALU = mybir.AluOpType
AX = mybir.AxisListType

D = 1024
KC = D // 128
NE = 12
DA = 64
PD = 256
TPC = 8
TW = TPC * 128
BIG = 1.0e30
INV_TAU = 1.0 / (1.0 + 1e-8)
LN_EPS = 1e-5
Z_EPS = 1e-8


class Sem:
    def __init__(self, nc, name):
        self.h = nc.semaphore(name).__enter__()
        self.cnt = 0

    def inc(self, instr, n=1):
        instr.then_inc(self.h, n)
        self.cnt += n
        return instr


class Builder:

    def __init__(self, nc, n_tok):
        self.nc = nc
        self.n_tok = n_tok
        self.n_tiles = n_tok // 128
        self.n_chunks = self.n_tiles // TPC
        t = {}
        self.t = t
        t["x"] = nc.dram_tensor("x", [n_tok, D], F32, kind="ExternalInput")
        t["o"] = nc.dram_tensor("o", [n_tok, D], F32, kind="ExternalOutput")
        t["xbuf"] = nc.dram_tensor("xbuf", [n_tok, D], F32, kind="Internal")
        t["w1t"] = nc.dram_tensor("w1t", [NE, D, D], F32, kind="ExternalInput")
        t["w2t"] = nc.dram_tensor("w2t", [NE, D, D], F32, kind="ExternalInput")
        t["pwt"] = nc.dram_tensor("pwt", [D, DA], F32, kind="ExternalInput")
        t["ut"] = nc.dram_tensor("ut", [DA, 3 * NE], F32, kind="ExternalInput")
        t["b2"] = nc.dram_tensor("b2", [NE, D], F32, kind="ExternalInput")
        t["qp"] = nc.dram_tensor("qp", [D, PD], F32, kind="ExternalInput")
        t["kp"] = nc.dram_tensor("kp", [D, PD], F32, kind="ExternalInput")
        t["ident"] = nc.dram_tensor("ident", [128, 128], F32, kind="ExternalInput")
        sb = nc.alloc_sbuf_tensor
        self.xc = [sb("xc0", [128, TPC * D], F32), sb("xc1", [128, TPC * D], F32)]
        self.A = sb("A", [128, TPC * D], F32)
        self.Bhi = sb("Bhi", [128, KC * TW], BF16)
        self.Blo = sb("Blo", [128, KC * TW], BF16)
        self.wb = [sb("wb0", [128, 2048], F32), sb("wb1", [128, 2048], F32)]
        self.att = sb("att", [128, 2 * TW], F32)
        self.kvs = sb("kvs", [128, 2 * D], F32)
        self.ident_s = sb("ident_s", [128, 128], F32)
        self.pwhi = sb("pwhi", [128, KC * DA], BF16)
        self.pwlo = sb("pwlo", [128, KC * DA], BF16)
        self.ut_s = sb("ut_s", [DA, 3 * NE], F32)
        self.b2_s = sb("b2_s", [NE, D], F32)
        self.qphi = sb("qphi", [128, KC * PD], BF16)
        self.qplo = sb("qplo", [128, KC * PD], BF16)
        self.kphi = sb("kphi", [128, KC * PD], BF16)
        self.kplo = sb("kplo", [128, KC * PD], BF16)
        self.C1 = sb("C1", [128, TPC * NE], F32)
        self.C2 = sb("C2", [128, TPC * NE], F32)
        self.C3 = sb("C3", [128, TPC * NE], F32)
        self.C3T = sb("C3T", [NE, TPC * 128], F32)
        self.xa_s = sb("xa_s", [DA, TW], F32)
        self.z_s = sb("z_s", [128, TPC * 3 * NE], F32)
        self.mcol = sb("mcol", [128, self.n_tiles], F32)
        self.rcol = sb("rcol", [128, self.n_tiles], F32)
        self.zcol = sb("zcol", [128, TPC], F32)
        self.scr = sb("scr", [128, 16], F32)
        self.g12 = sb("g12", [128, 4 * TPC * 3], F32)
        self.e144 = sb("e144", [128, TPC * 3 * NE], F32)
        self.zt144 = sb("zt144", [128, TPC * 3 * NE], F32)
        self.ones = sb("ones", [128, 1], F32)
        self.gtmp = sb("gtmp", [128, 4 * 128], F32)
        self.pA = nc.alloc_psum_tensor("pA", [128, 2048], F32)
        self.pB = nc.alloc_psum_tensor("pB", [128, 2048], F32)
        self.sv = Sem(nc, "sv")
        self.ss = Sem(nc, "ss")
        self.st = Sem(nc, "st_t")
        self.sw = [Sem(nc, "sw0"), Sem(nc, "sw1")]
        self.sin = Sem(nc, "sin")
        self.sxc = [Sem(nc, "sxc0"), Sem(nc, "sxc1")]
        self.sst = [Sem(nc, "sst0"), Sem(nc, "sst1")]
        self.wb_free_at = [0, 0]
        self.xc_free = [[], []]
        self.xc_ready = [0, 0]

    def wv(self, eng, val=None):
        eng.wait_ge(self.sv.h, self.sv.cnt if val is None else val)

    def ws(self, eng, val=None):
        eng.wait_ge(self.ss.h, self.ss.cnt if val is None else val)

    def wt(self, eng, val=None):
        eng.wait_ge(self.st.h, self.st.cnt if val is None else val)

    def load_xc(self, src_dram, chunk):
        nc = self.nc
        p = chunk % 2
        for sem, val in self.xc_free[p]:
            nc.sync.wait_ge(sem, val)
        self.xc_free[p] = []
        rows = src_dram[chunk * TW:(chunk + 1) * TW, :]
        src = rows.rearrange("(t p) f -> p t f", p=128)
        dst = self.xc[p][:].rearrange("p (t f) -> p t f", f=D)
        self.sxc[p].inc(nc.sync.dma_start(out=dst, in_=src), 16)
        self.xc_ready[p] = self.sxc[p].cnt

    def store_xc(self, dst_dram, chunk):
        nc = self.nc
        p = chunk % 2
        nc.sync.wait_ge(self.sv.h, self.sv.cnt)
        rows = dst_dram[chunk * TW:(chunk + 1) * TW, :]
        dst = rows.rearrange("(t p) f -> p t f", p=128)
        src = self.xc[p][:].rearrange("p (t f) -> p t f", f=D)
        self.sst[p].inc(nc.sync.dma_start(out=dst, in_=src), 16)
        self.xc_free[p] = [(self.sst[p].h, self.sst[p].cnt)]

    def emit_loads(self):
        nc, t = self.nc, self.t
        self.sin.inc(nc.sync.dma_start(out=self.ident_s[:], in_=t["ident"][:]), 16)
        self.sin.inc(nc.sync.dma_start(out=self.ut_s[:], in_=t["ut"][:]), 16)
        self.sin.inc(nc.sync.dma_start(out=self.b2_s[:], in_=t["b2"][:]), 16)
        for eng in (nc.vector, nc.scalar, nc.tensor):
            eng.wait_ge(self.sin.h, self.sin.cnt)
        self.sv.inc(nc.vector.memset(self.ones[:], 1.0))

    def emit_prologue(self):
        nc, t = self.nc, self.t
        V, S = nc.vector, nc.scalar
        for name, hi, lo, cols, j in (("qp", self.qphi, self.qplo, KC * PD, PD),
                                      ("kp", self.kphi, self.kplo, KC * PD, PD),
                                      ("pwt", self.pwhi, self.pwlo, KC * DA, DA)):
            stage = self.A[:, 0:cols]
            src = t[name][:].rearrange("(kc p) j -> p kc j", p=128)
            nc.sync.wait_ge(self.sv.h, self.sv.cnt)
            nc.sync.wait_ge(self.ss.h, self.ss.cnt)
            self.sin.inc(nc.sync.dma_start(
                out=stage.rearrange("p (kc j) -> p kc j", j=j), in_=src), 16)
            S.wait_ge(self.sin.h, self.sin.cnt)
            self.ss.inc(S.copy(hi[:], stage))
            V.wait_ge(self.sin.h, self.sin.cnt)
            self.ws(V)
            self.sv.inc(V.tensor_sub(lo[:], stage, hi[:]))

    def emit_ln(self, chunk, save_stats=False, use_saved=False, a_free_t=None):
        nc = self.nc
        V, S = nc.vector, nc.scalar
        p = chunk % 2
        xv = self.xc[p]
        for tt in range(TPC):
            g = chunk * TPC + tt
            xt = xv[:, tt * D:(tt + 1) * D]
            At = self.A[:, tt * D:(tt + 1) * D]
            if use_saved:
                m = self.mcol[:, g:g + 1]
                r = self.rcol[:, g:g + 1]
                if tt == 0:
                    V.wait_ge(self.sxc[p].h, self.xc_ready[p])
                    self.wt(V, a_free_t)
                    self.ws(V)
                    self.wv(V)
                self.sv.inc(V.tensor_scalar(At, xt, m, r, ALU.subtract, ALU.mult))
                continue
            m = self.mcol[:, g:g + 1] if save_stats else self.scr[:, 0:1]
            r = self.rcol[:, g:g + 1] if save_stats else self.scr[:, 1:2]
            sumc = self.scr[:, 2:3]
            sqc = self.scr[:, 3:4]
            varc = self.scr[:, 4:5]
            msq = self.scr[:, 5:6]
            if tt == 0:
                V.wait_ge(self.sxc[p].h, self.xc_ready[p])
                self.wv(V)
            self.sv.inc(V.reduce_sum(sumc, xt, axis=AX.X))
            if tt == 0:
                S.wait_ge(self.sxc[p].h, self.xc_ready[p])
                self.wt(S, a_free_t)
            self.wv(S)
            self.ss.inc(S.activation(At, xt, AF.Square, accum_out=sqc))
            self.ws(V)
            self.wv(V)
            self.sv.inc(V.tensor_scalar_mul(m, sumc, 1.0 / D))
            self.wv(V)
            self.sv.inc(V.tensor_mul(msq, m, m))
            self.sv.inc(V.tensor_scalar(varc, sqc, 1.0 / D, LN_EPS, ALU.mult, ALU.add))
            self.wv(V)
            self.sv.inc(V.tensor_sub(varc, varc, msq))
            self.wv(S)
            self.ss.inc(S.activation(varc, varc, AF.Ln))
            self.ws(S)
            self.ss.inc(S.activation(r, varc, AF.Exp, scale=-0.5))
            self.ws(V)
            self.sv.inc(V.tensor_scalar(At, xt, m, r, ALU.subtract, ALU.mult))

    def emit_transpose_A_to_B(self, arena, slots):
        nc = self.nc
        T, S, V = nc.tensor, nc.scalar, nc.vector
        slot_free = [None] * len(slots)
        first = True
        for tt in range(TPC):
            for kc in range(KC):
                i = tt * KC + kc
                sl = i % len(slots)
                base = slots[sl]
                pview = arena[:, base: base + 128]
                src = self.A[:, tt * D + kc * 128: tt * D + (kc + 1) * 128]
                if first:
                    self.wv(T)
                    self.ws(T)
                    first = False
                elif slot_free[sl] is not None:
                    self.ws(T, slot_free[sl][0])
                    self.wv(T, slot_free[sl][1])
                self.st.inc(T.transpose(pview, src, self.ident_s[:]))
                self.wt(S)
                hi = self.Bhi[:, kc * TW + tt * 128: kc * TW + (tt + 1) * 128]
                lo = self.Blo[:, kc * TW + tt * 128: kc * TW + (tt + 1) * 128]
                self.ss.inc(S.copy(hi, pview))
                self.wt(V)
                self.ws(V)
                self.sv.inc(V.tensor_sub(lo, pview, hi))
                slot_free[sl] = (self.ss.cnt, self.sv.cnt)

    def emit_transpose_gelu_A_to_B(self, arena, slots):
        nc = self.nc
        T, S, V = nc.tensor, nc.scalar, nc.vector
        slot_free = [None] * len(slots)
        first = True
        for tt in range(TPC):
            for kc in range(KC):
                i = tt * KC + kc
                sl = i % len(slots)
                base = slots[sl]
                pview = arena[:, base: base + 128]
                gt = self.gtmp[:, sl * 128:(sl + 1) * 128]
                src = self.A[:, tt * D + kc * 128: tt * D + (kc + 1) * 128]
                if first:
                    self.wv(T)
                    self.ws(T)
                    first = False
                elif slot_free[sl] is not None:
                    self.ws(T, slot_free[sl][0])
                    self.wv(T, slot_free[sl][1])
                self.st.inc(T.transpose(pview, src, self.ident_s[:]))
                self.wt(S)
                hi = self.Bhi[:, kc * TW + tt * 128: kc * TW + (tt + 1) * 128]
                lo = self.Blo[:, kc * TW + tt * 128: kc * TW + (tt + 1) * 128]
                self.ss.inc(S.activation(gt, pview, AF.Gelu))
                self.ws(S)
                self.ss.inc(S.copy(hi, gt))
                self.ws(V)
                self.sv.inc(V.tensor_sub(lo, gt, hi))
                slot_free[sl] = (self.ss.cnt, self.sv.cnt)

    def _mm3(self, T, out, lhi, llo, rhi, rlo, start, stop):
        T.matmul(out, lhi, rhi, start=start, stop=False)
        T.matmul(out, lhi, rlo, start=False, stop=False)
        return T.matmul(out, llo, rhi, start=False, stop=stop)

    def emit_routing(self, chunk):
        nc = self.nc
        T, V, S = nc.tensor, nc.vector, nc.scalar
        self.ws(T)
        self.wv(T)
        ins = None
        for kc in range(KC):
            for th in range(2):
                xap = self.pB[0:DA, th * 512:(th + 1) * 512]
                ins = self._mm3(T, xap,
                                self.pwhi[:, kc * DA:(kc + 1) * DA],
                                self.pwlo[:, kc * DA:(kc + 1) * DA],
                                self.Bhi[:, kc * TW + th * 512: kc * TW + (th + 1) * 512],
                                self.Blo[:, kc * TW + th * 512: kc * TW + (th + 1) * 512],
                                start=(kc == 0), stop=(kc == KC - 1))
        self.st.inc(ins)
        self.wt(V)
        self.sv.inc(V.tensor_copy(self.xa_s[:], self.pB[0:DA, 0:TW]))
        zp = self.pB[:, 1024:1024 + TPC * 3 * NE]
        self.wv(T)
        for tt in range(TPC):
            for r in range(3):
                self.st.inc(T.matmul(zp[:, tt * 36 + r * NE: tt * 36 + (r + 1) * NE],
                                     self.xa_s[:, tt * 128:(tt + 1) * 128],
                                     self.ut_s[:, r * NE:(r + 1) * NE],
                                     start=True, stop=True))
        self.wt(V)
        self.sv.inc(V.tensor_copy(self.z_s[:], zp))
        NG = TPC * 3
        H = NG // 2
        vdone = [self.sv.cnt, self.sv.cnt]
        sdone = [self.ss.cnt, self.ss.cnt]

        def half(ap3, h):
            return ap3[:, h * H:(h + 1) * H, :]

        z3 = self.z_s[:].rearrange("p (g w) -> p g w", w=NE)
        e3 = self.e144[:].rearrange("p (g w) -> p g w", w=NE)
        zt3 = self.zt144[:].rearrange("p (g w) -> p g w", w=NE)
        M1 = self.g12[:, 0:NG]
        M3 = self.g12[:, NG:2 * NG]
        DEN = self.g12[:, 2 * NG:3 * NG]
        RD = self.g12[:, 3 * NG:4 * NG]
        M1b = M1.unsqueeze(2).broadcast_to([128, NG, NE])
        M3b = M3.unsqueeze(2).broadcast_to([128, NG, NE])

        def vstep(fn):
            for h in (0, 1):
                self.wv(V, vdone[h])
                self.ws(V, sdone[h])
                self.sv.inc(fn(h))
                vdone[h] = self.sv.cnt

        def hs(col):
            return lambda h: col[:, h * H:(h + 1) * H]

        vstep(lambda h: V.reduce_max(hs(M1)(h), half(z3, h), axis=AX.X))
        vstep(lambda h: V.tensor_tensor(half(e3, h), half(z3, h), half(M1b, h), ALU.is_ge))
        vstep(lambda h: V.scalar_tensor_tensor(half(zt3, h), half(e3, h), -BIG,
                                               half(z3, h), ALU.mult, ALU.add))
        vstep(lambda h: V.reduce_max(hs(M3)(h), half(zt3, h), axis=AX.X))
        vstep(lambda h: V.tensor_tensor(half(e3, h), half(zt3, h), half(M3b, h), ALU.is_ge))
        vstep(lambda h: V.scalar_tensor_tensor(half(zt3, h), half(e3, h), -BIG,
                                               half(zt3, h), ALU.mult, ALU.add))
        vstep(lambda h: V.reduce_max(hs(M3)(h), half(zt3, h), axis=AX.X))
        vstep(lambda h: V.tensor_tensor(half(zt3, h), half(z3, h), half(M1b, h), ALU.subtract))
        vstep(lambda h: V.tensor_tensor(half(e3, h), half(z3, h), half(M3b, h), ALU.is_ge))
        for h in (0, 1):
            S.wait_ge(self.sv.h, vdone[h])
            zt_h = self.zt144[:, h * H * NE:(h + 1) * H * NE]
            self.ss.inc(S.activation(zt_h, zt_h, AF.Exp, scale=INV_TAU))
            sdone[h] = self.ss.cnt
        vstep(lambda h: V.tensor_mul(self.zt144[:, h * H * NE:(h + 1) * H * NE],
                                     self.zt144[:, h * H * NE:(h + 1) * H * NE],
                                     self.e144[:, h * H * NE:(h + 1) * H * NE]))
        vstep(lambda h: V.reduce_sum(hs(DEN)(h), half(zt3, h), axis=AX.X))
        vstep(lambda h: V.reciprocal(hs(RD)(h), hs(DEN)(h)))
        vstep(lambda h: V.tensor_scalar_mul(hs(DEN)(h), hs(RD)(h), 0.5))
        for h in (0, 1):
            self.wv(V, vdone[h])
        zt4 = self.zt144[:].rearrange("p (t r w) -> p t r w", r=3, w=NE)
        for r in range(3):
            C = (self.C1, self.C2, self.C3)[r]
            sc = RD if r == 0 else DEN
            sc_r = sc.rearrange("p (t r) -> p t r", r=3)[:, :, r]
            sc_rb = sc_r.unsqueeze(2).broadcast_to([128, TPC, NE])
            self.sv.inc(V.tensor_tensor(
                C[:].rearrange("p (t w) -> p t w", w=NE), zt4[:, :, r, :], sc_rb,
                ALU.mult))

    def load_quarter(self, wdram, pair, q):
        nc = self.nc
        S, V = nc.scalar, nc.vector
        e, oh = pair // 2, pair % 2
        kb, half = q // 2, q % 2
        gq = pair * 4 + q
        slot = gq % 2
        r0 = kb * 512 + half * 256
        src = wdram[e, r0:r0 + 256, oh * 512:(oh + 1) * 512]
        src = src.rearrange("(k p) o -> p k o", p=128)
        f32 = self.wb[slot][:, 0:1024]
        wbB = self.wb[slot][:].bitcast(BF16)
        hi = wbB[:, 2048:3072]
        lo = wbB[:, 3072:4096]
        nc.sync.wait_ge(self.st.h, self.wb_free_at[slot])
        self.sw[slot].inc(nc.sync.dma_start(
            out=f32.rearrange("p (k o) -> p k o", o=512), in_=src), 16)
        S.wait_ge(self.sw[slot].h, self.sw[slot].cnt)
        self.wt(S, self.wb_free_at[slot])
        self.ss.inc(S.copy(hi, f32))
        V.wait_ge(self.sw[slot].h, self.sw[slot].cnt)
        self.ws(V)
        self.wt(V, self.wb_free_at[slot])
        self.sv.inc(V.tensor_sub(lo, f32, hi))
        self.split_done[slot] = (self.ss.cnt, self.sv.cnt)

    def preload_mixture(self, wdram):
        self.split_done = [None, None]
        self.load_quarter(wdram, 0, 0)
        self.load_quarter(wdram, 0, 1)

    def emit_mixture(self, chunk, wdram, C, dest_is_xc, first_clears,
                     preloaded=False, hooks=None):
        nc = self.nc
        T, V = nc.tensor, nc.vector
        xv = self.xc[chunk % 2]

        self.wv(T)
        self.ws(T)
        if not preloaded:
            self.preload_mixture(wdram)
        evac_after = {}
        for pair in range(2 * NE):
            e, oh = pair // 2, pair % 2
            for q in range(4):
                gq = pair * 4 + q
                slot = gq % 2
                kb, half = q // 2, q % 2
                wbB = self.wb[slot][:].bitcast(BF16)
                sd = self.split_done[slot]
                self.ws(T, sd[0])
                self.wv(T, sd[1])
                for th in range(2):
                    grp = self.pA if th == 0 else self.pB
                    if q == 0 and pair >= 1 and (pair - 1, th) in evac_after:
                        T.wait_ge(self.sv.h, evac_after[(pair - 1, th)])
                    last = None
                    for k4h in range(2):
                        kc = kb * 4 + half * 2 + k4h
                        whi = wbB[:, 2048 + k4h * 512: 2048 + (k4h + 1) * 512]
                        wlo = wbB[:, 3072 + k4h * 512: 3072 + (k4h + 1) * 512]
                        for t4 in range(4):
                            tt = th * 4 + t4
                            bhi = self.Bhi[:, kc * TW + tt * 128: kc * TW + (tt + 1) * 128]
                            blo = self.Blo[:, kc * TW + tt * 128: kc * TW + (tt + 1) * 128]
                            last = self._mm3(T, grp[:, t4 * 512:(t4 + 1) * 512],
                                             bhi, blo, whi, wlo,
                                             start=(kc == 0), stop=(kc == KC - 1))
                    self.st.inc(last)
                    if q == 3:
                        self.wt(V)
                        lastv = None
                        for t4 in range(4):
                            tt = th * 4 + t4
                            ps = grp[:, t4 * 512:(t4 + 1) * 512]
                            if dest_is_xc:
                                dst = xv[:, tt * D + oh * 512: tt * D + oh * 512 + 512]
                            else:
                                dst = self.A[:, tt * D + oh * 512: tt * D + oh * 512 + 512]
                            col = C[:, tt * NE + e: tt * NE + e + 1]
                            if first_clears and e == 0:
                                lastv = V.tensor_scalar(dst, ps, col, None, ALU.mult)
                            else:
                                lastv = V.scalar_tensor_tensor(dst, ps, col, dst,
                                                               ALU.mult, ALU.add)
                        self.sv.inc(lastv)
                        evac_after[(pair, th)] = self.sv.cnt
                self.wb_free_at[slot] = self.st.cnt
                if gq + 2 < 2 * NE * 4:
                    npair, nq = (gq + 2) // 4, (gq + 2) % 4
                    self.load_quarter(wdram, npair, nq)
            if hooks and pair in hooks:
                hooks[pair]()

    def emit_b2_term(self, chunk):
        nc = self.nc
        T, V, S = nc.tensor, nc.vector, nc.scalar
        xv = self.xc[chunk % 2]
        self.wv(T)
        self.ws(T)
        for tt in range(TPC):
            b = 1024 if tt % 2 == 0 else 1536
            pv = self.pB[0:NE, b: b + 128]
            if tt >= 2:
                self.ws(T)
            self.st.inc(T.transpose(pv, self.C3[:, tt * NE:(tt + 1) * NE], self.ident_s[:]))
            self.wt(S)
            self.ss.inc(S.copy(self.C3T[:, tt * 128:(tt + 1) * 128], pv))
        self.ws(T)
        add_done = {}
        for tt in range(TPC):
            for dh in range(2):
                i = tt * 2 + dh
                pv = self.pA[:, (i % 4) * 512: (i % 4) * 512 + 512]
                if i >= 4:
                    self.wv(T, add_done[i - 4])
                self.st.inc(T.matmul(pv, self.C3T[:, tt * 128:(tt + 1) * 128],
                                     self.b2_s[:, dh * 512:(dh + 1) * 512],
                                     start=True, stop=True))
                self.wt(V)
                dst = xv[:, tt * D + dh * 512: tt * D + dh * 512 + 512]
                self.sv.inc(V.tensor_add(dst, dst, pv))
                add_done[i] = self.sv.cnt

    def emit_ffn_chunk(self, chunk, src_dram, dst_dram, ln_done=False):
        if chunk + 1 < self.n_chunks:
            self.load_xc(src_dram, chunk + 1)
        if not ln_done:
            self.emit_ln(chunk)
        self.emit_transpose_A_to_B(self.pA, [0, 512, 1024, 1536])
        self.preload_mixture(self.t["w1t"][:])
        self.emit_routing(chunk)
        self.emit_mixture(chunk, self.t["w1t"][:], self.C1,
                          dest_is_xc=False, first_clears=True, preloaded=True)
        self.emit_transpose_gelu_A_to_B(self.pA, [0, 512, 1024, 1536])
        self.emit_b2_term(chunk)
        self.preload_mixture(self.t["w2t"][:])
        hooks = None
        if chunk + 1 < self.n_chunks:
            hooks = {2: lambda: self.emit_ln(chunk + 1)}
        self.emit_mixture(chunk, self.t["w2t"][:], self.C2,
                          dest_is_xc=True, first_clears=False, preloaded=True,
                          hooks=hooks)
        self.store_xc(dst_dram, chunk)

    def emit_ffn(self, src_dram, dst_dram, first_ready=False):
        if not first_ready:
            self.load_xc(src_dram, 0)
        for c in range(self.n_chunks):
            self.emit_ffn_chunk(c, src_dram, dst_dram,
                                ln_done=(c > 0 or first_ready))

    def emit_attn(self):
        nc = self.nc
        T, V, S = nc.tensor, nc.vector, nc.scalar
        t = self.t
        kprime = self.att[:, 0:TPC * PD]
        kv_s0 = self.kvs[:, 0:D]
        kv_s1 = self.kvs[:, D:2 * D]
        ccol = self.scr[:, 10:12]
        self.wv(nc.vector)
        self.sv.inc(V.memset(ccol, 0.0))
        self.load_xc(t["xbuf"][:], 0)
        for c in range(self.n_chunks):
            p = c % 2
            xv = self.xc[p]
            if c + 1 < self.n_chunks:
                self.load_xc(t["xbuf"][:], c + 1)
            if c == 0:
                self.emit_ln(c, save_stats=True)
            self.emit_transpose_A_to_B(self.pB, [0, 512, 1024, 1536])
            a_free = self.st.cnt
            self._attn_afree = a_free
            self.ws(T)
            self.wv(T)
            kp_done = {}
            for th in range(2):
                last = None
                for t4 in range(4):
                    tt = th * 4 + t4
                    pv = self.pB[:, t4 * PD:(t4 + 1) * PD]
                    if th == 1:
                        self.ws(T, kp_done[0])
                    for kc in range(KC):
                        last = self._mm3(
                            T, pv,
                            self.Bhi[:, kc * TW + tt * 128: kc * TW + (tt + 1) * 128],
                            self.Blo[:, kc * TW + tt * 128: kc * TW + (tt + 1) * 128],
                            self.kphi[:, kc * PD:(kc + 1) * PD],
                            self.kplo[:, kc * PD:(kc + 1) * PD],
                            start=(kc == 0), stop=(kc == KC - 1))
                    self.st.inc(last)
                self.wt(S)
                for t4 in range(4):
                    tt = th * 4 + t4
                    self.ss.inc(S.activation(kprime[:, tt * PD:(tt + 1) * PD],
                                             self.pB[:, t4 * PD:(t4 + 1) * PD], AF.Exp))
                kp_done[th] = self.ss.cnt
            self.ws(V)
            lastv = None
            for tt in range(TPC):
                g = c * TPC + tt
                lastv = V.tensor_scalar(kprime[:, tt * PD:(tt + 1) * PD],
                                        kprime[:, tt * PD:(tt + 1) * PD],
                                        self.rcol[:, g:g + 1], None, ALU.mult)
            self.sv.inc(lastv)
            self.ws(T)
            self.wv(T)
            last = None
            for ec in range(2):
                for dh in range(2):
                    pv = self.pA[:, (ec * 2 + dh) * 512:(ec * 2 + dh + 1) * 512]
                    for tt in range(TPC):
                        last = T.matmul(pv,
                                        kprime[:, tt * PD + ec * 128: tt * PD + ec * 128 + 128],
                                        xv[:, tt * D + dh * 512: tt * D + dh * 512 + 512],
                                        start=(c == 0 and tt == 0),
                                        stop=(c == self.n_chunks - 1 and tt == TPC - 1))
            self.st.inc(last)
            last = None
            for ec in range(2):
                cps = self.pB[:, ec * 2: ec * 2 + 1]
                for tt in range(TPC):
                    g = c * TPC + tt
                    last = T.matmul(cps,
                                    kprime[:, tt * PD + ec * 128: tt * PD + ec * 128 + 128],
                                    self.mcol[:, g:g + 1],
                                    start=(tt == 0), stop=(tt == TPC - 1))
            self.st.inc(last)
            kv_read_done = self.st.cnt
            if c + 1 < self.n_chunks:
                self.emit_ln(c + 1, save_stats=True, a_free_t=a_free)
            self.wt(V)
            self.sv.inc(V.tensor_add(ccol[:, 0:1], ccol[:, 0:1], self.pB[:, 0:1]))
            self.sv.inc(V.tensor_add(ccol[:, 1:2], ccol[:, 1:2], self.pB[:, 2:3]))
            self.wv(V)
            self.xc_free[p] = [(self.st.h, kv_read_done)]
        self.load_xc(t["xbuf"][:], 0)
        self.emit_ln(0, use_saved=True, a_free_t=self._attn_afree)
        self.wt(V)
        self.sv.inc(V.tensor_copy(kv_s0, self.pA[:, 0:1024]))
        self.sv.inc(V.tensor_copy(kv_s1, self.pA[:, 1024:2048]))
        self.wv(V)
        self.sv.inc(V.tensor_scalar(kv_s0, kv_s0, ccol[:, 0:1], None, ALU.subtract))
        self.sv.inc(V.tensor_scalar(kv_s1, kv_s1, ccol[:, 1:2], None, ALU.subtract))
        for c in range(self.n_chunks):
            p = c % 2
            xv = self.xc[p]
            if c + 1 < self.n_chunks:
                self.load_xc(t["xbuf"][:], c + 1)
            self.emit_transpose_A_to_B(self.pA, [0, 512, 1024, 1536])
            self._attn_afree = self.st.cnt
            self.ws(T)
            self.wv(T)
            q_done = {}
            for ec in range(2):
                for th in range(2):
                    i = ec * 2 + th
                    pv = self.pB[:, (i % 2) * 512:(i % 2) * 512 + 512]
                    if i >= 2:
                        self.ws(T, q_done[i - 2])
                    last = None
                    for kc in range(KC):
                        last = self._mm3(
                            T, pv,
                            self.qphi[:, kc * PD + ec * 128: kc * PD + ec * 128 + 128],
                            self.qplo[:, kc * PD + ec * 128: kc * PD + ec * 128 + 128],
                            self.Bhi[:, kc * TW + th * 512: kc * TW + (th + 1) * 512],
                            self.Blo[:, kc * TW + th * 512: kc * TW + (th + 1) * 512],
                            start=(kc == 0), stop=(kc == KC - 1))
                    self.st.inc(last)
                    self.wt(S)
                    self.ss.inc(S.activation(
                        self.att[:, ec * TW + th * 512: ec * TW + (th + 1) * 512],
                        pv, AF.Exp))
                    q_done[i] = self.ss.cnt
            self.ws(T)
            last = None
            for tt in range(TPC):
                pv = self.pB[:, 1024 + tt * 8: 1024 + tt * 8 + 1]
                for ec in range(2):
                    last = T.matmul(pv,
                                    self.att[:, ec * TW + tt * 128: ec * TW + (tt + 1) * 128],
                                    self.ones[:], start=(ec == 0), stop=(ec == 1))
            self.st.inc(last)
            self.wt(V)
            for tt in range(TPC):
                zc = self.zcol[:, tt:tt + 1]
                self.sv.inc(V.tensor_scalar_add(
                    zc, self.pB[:, 1024 + tt * 8: 1024 + tt * 8 + 1], Z_EPS))
            self.wv(V)
            for tt in range(TPC):
                zc = self.zcol[:, tt:tt + 1]
                self.sv.inc(V.reciprocal(zc, zc))
            if c + 1 < self.n_chunks:
                self.emit_ln(c + 1, use_saved=True)
            else:
                self.load_xc(t["xbuf"][:], 0)
                self.emit_ln(0, a_free_t=self._attn_afree)
            num_done = {}
            for tt in range(TPC):
                for dh in range(2):
                    i = tt * 2 + dh
                    base = 1536 if i % 2 == 0 else 1024
                    pv = self.pB[:, base: base + 512]
                    if i < 2:
                        self.wv(T)
                    else:
                        self.wv(T, num_done[i - 2])
                    for ec in range(2):
                        last = T.matmul(pv,
                                        self.att[:, ec * TW + tt * 128: ec * TW + (tt + 1) * 128],
                                        (kv_s0 if ec == 0 else kv_s1)[:, dh * 512:(dh + 1) * 512],
                                        start=(ec == 0), stop=(ec == 1))
                    self.st.inc(last)
                    self.wt(V)
                    dst = xv[:, tt * D + dh * 512: tt * D + dh * 512 + 512]
                    self.sv.inc(V.scalar_tensor_tensor(dst, pv, self.zcol[:, tt:tt + 1],
                                                       dst, ALU.mult, ALU.add))
                    num_done[i] = self.sv.cnt
            self.store_xc(t["xbuf"][:], c)

    def emit_all(self):
        t = self.t
        self.emit_loads()
        self.emit_prologue()
        self.emit_ffn(t["x"][:], t["xbuf"][:])
        self.emit_attn()
        self.emit_ffn(t["xbuf"][:], t["o"][:], first_ready=True)
        nc = self.nc
        for s in self.sst:
            nc.sync.wait_ge(s.h, s.cnt)


def build_program(n_tok=4096):
    nc = bass.Bass("TRN2", target_bir_lowering=False, debug=False)
    b = Builder(nc, n_tok)
    b.emit_all()
    return nc


def prep_inputs(L0_attn_q_proj, L0_attn_k_proj, L0_ffn_P_w, L0_ffn_U1, L0_ffn_U2,
                L0_ffn_U3, L0_ffn_W1, L0_ffn_W2, L0_ffn_b2):
    c = np.ascontiguousarray
    f = np.float32
    return {
        "w1t": c(np.asarray(L0_ffn_W1, dtype=f).transpose(0, 2, 1)),
        "w2t": c(np.asarray(L0_ffn_W2, dtype=f).transpose(0, 2, 1)),
        "pwt": c(np.asarray(L0_ffn_P_w, dtype=f).T),
        "ut": c(np.concatenate([np.asarray(L0_ffn_U1, dtype=f).T,
                                np.asarray(L0_ffn_U2, dtype=f).T,
                                np.asarray(L0_ffn_U3, dtype=f).T], axis=1)),
        "b2": c(np.asarray(L0_ffn_b2, dtype=f)),
        "qp": c(np.asarray(L0_attn_q_proj, dtype=f)),
        "kp": c(np.asarray(L0_attn_k_proj, dtype=f)),
        "ident": np.eye(128, dtype=f),
    }


def kernel(L0_attn_x, L0_attn_q_proj, L0_attn_k_proj, L0_ffn_P_w, L0_ffn_U1,
           L0_ffn_U2, L0_ffn_U3, L0_ffn_W1, L0_ffn_W2, L0_ffn_b2):
    x = np.ascontiguousarray(np.asarray(L0_attn_x, dtype=np.float32))
    B = x.shape[0]
    shared = prep_inputs(L0_attn_q_proj, L0_attn_k_proj, L0_ffn_P_w, L0_ffn_U1,
                         L0_ffn_U2, L0_ffn_U3, L0_ffn_W1, L0_ffn_W2, L0_ffn_b2)
    nc = build_program(n_tok=x.shape[1])
    in_maps = [dict(shared, x=np.ascontiguousarray(x[c])) for c in range(B)]
    res = run_bass_kernel_spmd(nc, in_maps, core_ids=list(range(B)))
    return np.stack([res.results[c]["o"] for c in range(B)], axis=0)
